# version 11
# baseline (speedup 1.0000x reference)
import numpy as np
from contextlib import ExitStack

try:
    import concourse.bass as bass
except ImportError:
    import sys
    sys.path.insert(0, "/opt/trn_rl_repo")
    import concourse.bass as bass
import concourse.mybir as mybir
from concourse.bass_utils import run_bass_kernel_spmd

F32 = mybir.dt.float32
BF16 = mybir.dt.bfloat16
AF = mybir.ActivationFunctionType
ALU = mybir.AluOpType
AX = mybir.AxisListType

NCORE = 8
T = 2048
DM = 1024
NB = 16
NOCT = 8
NFC = 8
NEXP = 8
EPS = 1e-5
INV_DM = 1.0 / DM


class Prog:

    ENGS = ("tensor", "vector", "scalar", "gpsimd", "sync")

    def __init__(self, nc, sems, gap_dummy):
        self.nc = nc
        self.sems = sems
        self.count = {k: 0 for k in sems}
        self.gap = gap_dummy
        self._reset()

    def _reset(self):
        self.ops = {e: [] for e in self.ENGS}
        self.written = {e: set() for e in self.ENGS}
        self.readk = {e: set() for e in self.ENGS}
        self.dirty = {e: False for e in self.ENGS}

    def emit(self, eng, fn, waits=(), inc=None, inc_n=1, reads=(), writes=()):
        if eng in ("vector", "scalar"):
            need = (any(r in self.written[eng] for r in reads)
                    or any(w in self.readk[eng] or w in self.written[eng]
                           for w in writes)
                    or inc is not None)
            if need:
                self.ops[eng].append((lambda e: e.drain(), (), None, 0))
                self.written[eng].clear()
                self.readk[eng].clear()
                self.dirty[eng] = False
        self.ops[eng].append((fn, tuple(waits), inc, inc_n))
        if inc is not None:
            self.count[inc] += inc_n
        elif eng in ("vector", "scalar"):
            self.dirty[eng] = True
        for wkey in writes:
            self.written[eng].add(wkey)
        for rkey in reads:
            self.readk[eng].add(rkey)
        return self.count[inc] if inc else None

    def val(self, sem):
        return self.count[sem]

    def flush_block(self):
        ops, sems = self.ops, self.sems

        def mk(eng_name):
            def run(eng):
                for fn, waits, inc, inc_n in ops[eng_name]:
                    for s, v in waits:
                        eng.wait_ge(sems[s], v)
                    ins = fn(eng)
                    if inc is not None:
                        ins.then_inc(sems[inc], inc_n)
            return run

        with self.nc.Block() as blk:
            blk.tensor(mk("tensor"))
            blk.vector(mk("vector"))
            blk.scalar(mk("scalar"))
            blk.gpsimd(mk("gpsimd"))
            blk.sync(mk("sync"))
        self._reset()


LNW = 20
G0 = 80
GC = {"gmax": 0, "gnm": 2, "gsum": 4, "grec": 6, "gws": 8, "gwr": 10}
PIDX = 94


def build_program():
    nc = bass.Bass("TRN2", target_bir_lowering=False)

    x_h = nc.dram_tensor("x", [T, DM], F32, kind="ExternalInput")
    wq_h = nc.dram_tensor("wq", [DM, DM], F32, kind="ExternalInput")
    wk_h = nc.dram_tensor("wk", [DM, DM], F32, kind="ExternalInput")
    wv_h = nc.dram_tensor("wv", [DM, DM], F32, kind="ExternalInput")
    wo_h = nc.dram_tensor("wo", [DM, DM], F32, kind="ExternalInput")
    bq_h = nc.dram_tensor("bq", [1, DM], F32, kind="ExternalInput")
    bk_h = nc.dram_tensor("bk", [1, DM], F32, kind="ExternalInput")
    bv_h = nc.dram_tensor("bv", [1, DM], F32, kind="ExternalInput")
    bo_h = nc.dram_tensor("bo", [1, DM], F32, kind="ExternalInput")
    gamma_h = nc.dram_tensor("gamma", [1, DM], F32, kind="ExternalInput")
    beta_h = nc.dram_tensor("beta", [1, DM], F32, kind="ExternalInput")
    gw_h = nc.dram_tensor("gw", [DM, NEXP], F32, kind="ExternalInput")
    gb_h = nc.dram_tensor("gb", [1, NEXP], F32, kind="ExternalInput")
    eb_h = nc.dram_tensor("eb", [NEXP, DM], F32, kind="ExternalInput")
    ew_h = nc.dram_tensor("ew", [NEXP, DM, DM], F32, kind="ExternalInput")
    c_h = nc.dram_tensor("c", [1, DM], F32, kind="ExternalInput")
    wmod_h = nc.dram_tensor("wmod", [DM, 6 * DM], F32, kind="ExternalInput")
    out_h = nc.dram_tensor("out", [T, DM], F32, kind="ExternalOutput")
    wbh = [nc.dram_tensor("wbh%d" % k, [128, NFC * DM], BF16, kind="Internal")
           for k in range(4)]

    es = ExitStack()
    sb = lambda name, shape, dt: es.enter_context(nc.sbuf_tensor(name, shape, dt))

    xsb = sb("xsb", [128, NB * DM], F32)
    h2T = sb("h2T", [128, NFC * T], BF16)
    CAb = sb("CAb", [128, DM], BF16)
    SAb = sb("SAb", [128, DM], BF16)
    CBb = sb("CBb", [128, DM], BF16)
    SBb = sb("SBb", [128, DM], BF16)
    GBb = sb("GBb", [128, DM], BF16)
    GAMGA = sb("GAMGA", [128, DM], BF16)
    BETGA = sb("BETGA", [128, DM], BF16)
    identf = sb("identf", [128, 128], F32)
    ones1f = sb("ones1f", [1, 128], F32)
    ones1b = sb("ones1b", [1, 128], BF16)
    borow_bf = sb("borow_bf", [1, DM], BF16)
    onesb = sb("onesb", [128, 64], BF16)
    onescol = sb("onescol", [128, 1], F32)
    bqcol = sb("bqcol", [128, NFC], F32)
    bkcol = sb("bkcol", [128, NFC], F32)
    bvcol_bf = sb("bvcol_bf", [128, NFC], BF16)
    creshape = sb("creshape", [128, NFC], F32)
    s_col = sb("s_col", [128, NFC], F32)
    gw_bf = sb("gw_bf", [128, NFC * NEXP], BF16)
    GBIASb = sb("GBIASb", [128, NEXP], F32)
    w_gate = sb("w_gate", [128, NB * NEXP], F32)
    wT_bf = sb("wT_bf", [NEXP, T], BF16)
    stats = sb("stats", [128, 96], F32)
    gatef = sb("gatef", [128, NEXP], F32)
    gatee = sb("gatee", [128, NEXP], F32)
    gEM = sb("gEM", [NEXP, 128], F32)
    c0125 = sb("c0125", [128, 1], F32)
    cm3 = sb("cm3", [128, 1], F32)
    cn1 = sb("cn1", [128, 1], F32)
    ceps = sb("ceps", [128, 1], F32)
    cmhalf = sb("cmhalf", [128, 1], F32)
    recipB = sb("recipB", [128, DM], BF16)
    recrow = recipB
    rowB = sb("rowB", [1, DM], F32)

    sems = {}
    for name in ("dx", "dw", "dw2", "dsc", "dsm", "dsg", "dsh", "dew0", "dew1", "dout", "dwb0", "dwb1", "dwh0", "dwh1",
                 "t1", "t2", "v1", "v2", "s1", "s2", "m1", "m2"):
        sems[name] = es.enter_context(nc.semaphore(name))

    pbA = es.enter_context(nc.psum_tensor("pbA", [128, DM], F32))
    pbB = es.enter_context(nc.psum_tensor("pbB", [128, DM], F32))
    pbC = es.enter_context(nc.psum_tensor("pbC", [128, DM], F32))
    pbD = es.enter_context(nc.psum_tensor("pbD", [128, DM], F32))

    P = Prog(nc, sems, None)

    for tt in range(NB):
        P.emit("sync", lambda e, tt=tt: e.dma_start(
            xsb[:, DM * tt:DM * (tt + 1)], x_h.ap()[128 * tt:128 * (tt + 1), :]),
            inc="dx", inc_n=16)

    P.emit("sync", lambda e: e.dma_start(
        creshape[:, :], c_h.ap().rearrange("a (f p) -> p (f a)", p=128),
        allow_slow_non_contiguous=True),
        inc="dsc", inc_n=16)
    P.emit("sync", lambda e: e.dma_start(
        bqcol[:, :], bq_h.ap().rearrange("a (f p) -> p (f a)", p=128),
        allow_slow_non_contiguous=True),
        inc="dsm", inc_n=16)
    P.emit("sync", lambda e: e.dma_start(
        bkcol[:, :], bk_h.ap().rearrange("a (f p) -> p (f a)", p=128),
        allow_slow_non_contiguous=True),
        inc="dsm", inc_n=16)
    P.emit("gpsimd", lambda e: e.dma_start(
        bvcol_bf[:, :], bv_h.ap().rearrange("a (f p) -> p (f a)", p=128),
        allow_slow_non_contiguous=True),
        inc="dsg", inc_n=16)
    DSM_TOTAL = 96
    P.emit("gpsimd", lambda e: e.dma_start(
        gw_bf[:, :].rearrange("p (f g) -> p f g", f=NFC),
        gw_h.ap().rearrange("(f p) g -> p f g", p=128)), inc="dsg", inc_n=16)
    V_DSM = DSM_TOTAL
    V_DSG = 32

    for ms in (lambda e: e.memset(ones1b[:, :], 1.0),
               lambda e: e.memset(ones1f[:, :], 1.0),
               lambda e: e.memset(onesb[:, :], 1.0),
               lambda e: e.memset(onescol[:, :], 1.0),
               lambda e: e.memset(c0125[:, :], 0.125),
               lambda e: e.memset(cm3[:, :], -3.0),
               lambda e: e.memset(cn1[:, :], -1.0),
               lambda e: e.memset(cmhalf[:, :], -0.5),
               lambda e: e.memset(ceps[:, :], EPS)):
        P.emit("gpsimd", ms)
    P.emit("gpsimd", lambda e: e.iota(stats[:, PIDX:PIDX + 1], [[0, 1]],
                                      channel_multiplier=1,
                                      allow_small_or_imprecise_dtypes=True))
    P.emit("gpsimd", lambda e: e.iota(identf[:, :], [[1, 128]], channel_multiplier=0,
                                      allow_small_or_imprecise_dtypes=True), inc="m1")
    vm = P.val("m1")
    P.emit("vector", lambda e: e.tensor_scalar(
        identf[:, :], identf[:, :], stats[:, PIDX:PIDX + 1], None, ALU.is_equal),
        waits=[("m1", vm)], reads=("identf",), writes=("identf",), inc="v1")

    P.emit("scalar", lambda e: e.activation(s_col[:, :], creshape[:, :], AF.Silu),
           waits=[("dsc", 16)], inc="s1")
    S_SILU = P.val("s1")

    with nc.sbuf_tensor("wmstage", [128, 3 * DM], F32) as wmstage, \
         nc.sbuf_tensor("macc", [128, 3 * DM], F32) as macc, \
         nc.sbuf_tensor("gamb", [128, DM], F32) as gamb, \
         nc.sbuf_tensor("gtmp", [128, DM], F32) as gtmp, \
         nc.sbuf_tensor("gs", [128, DM], F32) as gs, \
         nc.sbuf_tensor("mod_row", [1, 6 * DM], F32) as mod_row, \
         nc.sbuf_tensor("grow", [1, NEXP], F32) as grow:
        psmod = pbA[0:1, :]
        psbcb = pbB[:, :]

        P.emit("sync", lambda e: e.dma_start(grow[:, :], gb_h.ap()[:, :]),
               inc="dsm", inc_n=16)

        HW = 1536
        V_RED0 = 0
        kkg = 0
        vstt_hist = {}
        for half in range(2):
            hbase = 3 * DM * half
            for fc in range(NFC):
                for hf in range(2):
                    buf = (kkg % 2) * HW
                    dws = "dw" if kkg % 2 == 0 else "dw2"
                    tw = [("v1", vstt_hist[kkg - 2])] if kkg >= 2 else []
                    P.emit("sync", lambda e, fc=fc, hf=hf, buf=buf, hbase=hbase: e.dma_start(
                        wmstage[:, buf:buf + HW],
                        wmod_h.ap()[128 * fc:128 * (fc + 1),
                                    hbase + HW * hf:hbase + HW * (hf + 1)]),
                        waits=tw, inc=dws, inc_n=16)
                    dwv = (dws, P.val(dws))
                    key = "macc%d" % hf
                    if fc == 0:
                        P.emit("vector", lambda e, hf=hf, buf=buf, fc=fc: e.tensor_scalar(
                            macc[:, HW * hf:HW * (hf + 1)], wmstage[:, buf:buf + HW],
                            s_col[:, fc:fc + 1], None, ALU.mult),
                            waits=[dwv, ("s1", S_SILU), ("t1", V_RED0)],
                            writes=(key,))
                    else:
                        P.emit("vector", lambda e, hf=hf, buf=buf, fc=fc: e.scalar_tensor_tensor(
                            macc[:, HW * hf:HW * (hf + 1)], wmstage[:, buf:buf + HW],
                            s_col[:, fc:fc + 1], macc[:, HW * hf:HW * (hf + 1)],
                            ALU.mult, ALU.add),
                            waits=[dwv], reads=(key,), writes=(key,))
                    P.emit("vector", lambda e: e.engine_nop(), inc="v1")
                    vstt_hist[kkg] = P.val("v1")
                    kkg += 1
            V_MACC = P.val("v1")
            for rnd in range(3):
                for j in range(2):
                    col = rnd * 1024 + j * 512
                    mw = []
                    if rnd == 0 and j == 0:
                        mw.append(("v1", V_MACC))
                    if (half > 0 or rnd > 0) and j == 0:
                        mw.append(("v2", P.val("v2")))
                    P.emit("tensor", lambda e, col=col, j=j: e.matmul(
                        psmod[:, 512 * j:512 * (j + 1)], onescol[:, :],
                        macc[:, col:col + 512], start=True, stop=True),
                        waits=mw, inc="t1")
                tv = P.val("t1")
                P.emit("vector", lambda e, rnd=rnd, hbase=hbase: e.tensor_copy(
                    mod_row[:, hbase + 1024 * rnd:hbase + 1024 * (rnd + 1)],
                    psmod[:, :]),
                    waits=[("t1", tv)], writes=("modrow",), inc="v2")
            P.emit("vector", lambda e: e.engine_nop(), inc="v1")
            if half == 0:
                V_RED0 = P.val("t1")
        V_MODROW = P.val("v1")

        def bcast1024(row_fn, waitv):
            P.emit("tensor", lambda e: e.matmul(
                psbcb[:, 0:512], ones1f[:, :], row_fn(0), start=True, stop=True),
                waits=waitv, inc="t1")
            return P.emit("tensor", lambda e: e.matmul(
                psbcb[:, 512:1024], ones1f[:, :], row_fn(1), start=True, stop=True),
                inc="t1")

        def mslice(base):
            return lambda h: mod_row[:, base + 512 * h: base + 512 * (h + 1)]

        tv = bcast1024(mslice(1 * DM), [("v1", V_MODROW)])
        P.emit("vector", lambda e: e.tensor_scalar(CAb[:, :], psbcb[:, :], 1.0, None, ALU.add),
               waits=[("t1", tv)], inc="v1")
        tv = bcast1024(mslice(0 * DM), [("v1", P.val("v1"))])
        P.emit("vector", lambda e: e.tensor_copy(SAb[:, :], psbcb[:, :]),
               waits=[("t1", tv)], inc="v1")
        tv = bcast1024(mslice(4 * DM), [("v1", P.val("v1"))])
        P.emit("vector", lambda e: e.tensor_scalar(CBb[:, :], psbcb[:, :], 1.0, None, ALU.add),
               waits=[("t1", tv)], inc="v1")
        tv = bcast1024(mslice(3 * DM), [("v1", P.val("v1"))])
        P.emit("vector", lambda e: e.tensor_copy(SBb[:, :], psbcb[:, :]),
               waits=[("t1", tv)], inc="v1")

        def signmax_chain():
            P.emit("vector", lambda e: e.tensor_scalar(gs[:, :], gtmp[:, :], -1.0, None, ALU.mult),
                   reads=("gtmp",), writes=("gs",))
            P.emit("vector", lambda e: e.tensor_max(gs[:, :], gs[:, :], gtmp[:, :]),
                   reads=("gs",), writes=("gs",))
            P.emit("vector", lambda e: e.tensor_scalar(gs[:, :], gs[:, :], 0.25, None, ALU.max),
                   reads=("gs",), writes=("gs",))
            P.emit("vector", lambda e: e.tensor_scalar(gtmp[:, :], gtmp[:, :], 0.0, None, ALU.is_ge),
                   writes=("gtmp",))
            P.emit("vector", lambda e: e.tensor_scalar(gtmp[:, :], gtmp[:, :], 2.0, -1.0, ALU.mult, ALU.add),
                   reads=("gtmp",), writes=("gtmp",))
            P.emit("vector", lambda e: e.tensor_mul(gs[:, :], gs[:, :], gtmp[:, :]),
                   reads=("gtmp",), writes=("gs",))

        tv = bcast1024(mslice(2 * DM), [("v1", P.val("v1"))])
        P.emit("vector", lambda e: e.tensor_copy(gtmp[:, :], psbcb[:, :]),
               waits=[("t1", tv)], writes=("gtmp",))
        signmax_chain()
        P.emit("sync", lambda e: e.dma_start(rowB[:, :], gamma_h.ap()[:, :]),
               inc="dsm", inc_n=16)
        P.emit("vector", lambda e: e.engine_nop(), inc="v1")
        tv = bcast1024(lambda h: rowB[:, 512 * h:512 * (h + 1)],
                       [("v1", P.val("v1")), ("dsm", 64)])
        P.emit("vector", lambda e: e.tensor_copy(gamb[:, :], psbcb[:, :]),
               waits=[("t1", tv)], writes=("gamb",))
        P.emit("vector", lambda e: e.tensor_mul(GAMGA[:, :], gs[:, :], gamb[:, :]),
               reads=("gamb",), inc="v1")
        P.emit("sync", lambda e: e.dma_start(rowB[:, :], beta_h.ap()[:, :]),
               waits=[("t1", tv)], inc="dsm", inc_n=16)
        T1_BETA = tv
        tv = bcast1024(lambda h: rowB[:, 512 * h:512 * (h + 1)],
                       [("v1", P.val("v1")), ("dsm", 80)])
        T1_BETA = tv
        P.emit("vector", lambda e: e.tensor_copy(gamb[:, :], psbcb[:, :]),
               waits=[("t1", tv)], writes=("gamb",))
        P.emit("vector", lambda e: e.tensor_mul(BETGA[:, :], gs[:, :], gamb[:, :]),
               reads=("gamb",), inc="v1")
        tv = bcast1024(mslice(5 * DM), [("v1", P.val("v1"))])
        P.emit("vector", lambda e: e.tensor_copy(gtmp[:, :], psbcb[:, :]),
               waits=[("t1", tv)], writes=("gtmp",))
        signmax_chain()
        P.emit("vector", lambda e: e.tensor_copy(GBb[:, :], gs[:, :]),
               reads=("gs",), inc="v1")
        tv = P.emit("tensor", lambda e: e.matmul(
            psbcb[:, 0:NEXP], ones1f[:, :], grow[:, :], start=True, stop=True),
            waits=[("v1", P.val("v1")), ("dsm", 80)], inc="t1")
        P.emit("vector", lambda e: e.tensor_copy(GBIASb[:, :], psbcb[:, 0:NEXP]),
               waits=[("t1", tv)], inc="v1")

        P.flush_block()

    es2 = ExitStack()
    sb2 = lambda name, shape, dt: es2.enter_context(nc.sbuf_tensor(name, shape, dt))
    wbuf0 = sb2("wbuf0", [128, NFC * DM], BF16)
    wbuf1 = sb2("wbuf1", [128, NFC * DM], BF16)
    wbuf = (wbuf0, wbuf1)
    h_bf = sb2("h_bf", [128, 3 * DM], BF16)
    h0slot = lambda o: (o % 2) * 2 * DM
    xnscr = sb2("xnscr", [128, 2 * DM], BF16)
    sqscr = xnscr
    preLN = sb2("preLN", [128, 2 * DM], BF16)
    hf32 = sb2("hf32", [128, DM], F32)
    h2scr = hf32
    Qasm = sb2("Qasm", [128, 2 * T], BF16)
    Kasm = sb2("Kasm", [128, 2 * T], BF16)
    Vbuf = sb2("Vbuf", [128, 2 * 16 * 65], BF16)
    ctxFM = sb2("ctxFM", [128, 2 * 2048], BF16)
    hT = sb2("hT", [128, 2048], BF16)
    PT = hT

    qview = lambda t: t[:, :].rearrange("p (h g s) -> p h g s", h=2, g=16, s=128)
    qview4 = lambda t: t[:, :].rearrange("p (h g2 pa s) -> p h g2 pa s",
                                         h=2, g2=8, pa=2, s=128)
    vview = Vbuf[:, :].rearrange("p (h g d) -> p h g d", h=2, g=16, d=65)
    cview = ctxFM[:, :].rearrange("p (r c h s) -> p r c h s", r=2, c=8, h=2, s=128)

    def emit_ln(src_fn, key, lane, out_bf_fn, src_key=None, extra_wait=None):
        base = LNW * lane
        c = lambda idx: stats[:, base + idx:base + idx + 1]
        w = ([] if extra_wait is None else [extra_wait])
        rd = (src_key,) if src_key else ()
        P.emit("vector", lambda e: e.bn_stats(stats[:, base:base + 6], src_fn()[:, 0:512]),
               waits=w, reads=rd, writes=(key + "b0",))
        P.emit("vector", lambda e: e.bn_stats(stats[:, base + 6:base + 12], src_fn()[:, 512:1024]),
               writes=(key + "b1",))
        P.emit("vector", lambda e: e.bn_aggr(stats[:, base + 12:base + 14],
                                             stats[:, base:base + 12]),
               reads=(key + "b0", key + "b1"), inc="v2")
        vv = P.val("v2")
        P.emit("scalar", lambda e: e.activation(c(14), c(13), AF.Ln, bias=ceps[:, :]),
               waits=[("v2", vv)], writes=(key + "lnv",))
        P.emit("scalar", lambda e: e.activation(c(15), c(14), AF.Exp, scale=cmhalf[:, :]),
               reads=(key + "lnv",), inc="s2")
        sv = P.val("s2")
        P.emit("vector", lambda e: e.scalar_tensor_tensor(
            c(16), c(12), -1.0, c(15), ALU.mult, ALU.mult),
            waits=[("s2", sv)], writes=(key + "nmr",))
        P.emit("vector", lambda e: e.tensor_scalar(
            out_bf_fn(), src_fn(), c(15), c(16), ALU.mult, ALU.add),
            reads=(key + "nmr",) + rd, inc="v1")
        return ("v1", P.val("v1"))

    ln1_sv = {}
    HD0 = {}

    def emit_h0(o, extra_t1=None):
        hw = [ln1_sv[o][0]]
        if extra_t1 is not None:
            hw.append(("t1", extra_t1))
        P.emit("vector", lambda e: e.tensor_mul(
            hf32[:, 0:DM], Kasm[:, 0:DM], CAb[:, :]),
            waits=hw, writes=("hf",))
        P.emit("vector", lambda e: e.tensor_add(
            hf32[:, 0:DM], hf32[:, 0:DM], SAb[:, :]),
            reads=("hf",), writes=("hf",), inc="v1")
        P.emit("vector", lambda e, o=o: e.tensor_copy(
            h_bf[:, h0slot(o):h0slot(o) + DM], hf32[:, 0:DM]),
            reads=("hf",), inc="v1")
        HD0[o] = P.val("v1")

    def emit_ln1_pair(o):
        svs = []
        for i in range(2):
            tt = 2 * o + i
            sv = emit_ln(lambda tt=tt: xsb[:, DM * tt:DM * (tt + 1)],
                         "ln1_%d" % i, 2 + i,
                         lambda i=i: Kasm[:, DM * i:DM * (i + 1)],
                         extra_wait=(("dx", 16 * NB) if o == 0 else None))
            svs.append(sv)
        ln1_sv[o] = svs

    if True:
        psbv = pbC[0:1, :]
        for k, w_hh in enumerate((wq_h, wk_h, wv_h, wo_h)):
            slot = k % 2
            dwbs = "dwb%d" % slot
            tw = [("dwh%d" % slot, 16 * (k // 2))] if k >= 2 else []
            P.emit("gpsimd", lambda e, w_hh=w_hh, slot=slot: e.dma_start(
                wbuf[slot][:, :].rearrange("p (f o) -> p f o", f=NFC),
                w_hh.ap().rearrange("(f p) o -> p f o", p=128)),
                waits=tw, inc=dwbs, inc_n=16)
            dv = P.val(dwbs)
            P.emit("sync", lambda e, k=k, slot=slot: e.dma_start(
                wbh[k].ap()[:, :], wbuf[slot][:, :]),
                waits=[(dwbs, dv)], inc="dwh%d" % slot, inc_n=16)
        V_WPRE = 32

        n = 0
        for fc in range(NFC):
            for hh in range(2):
                n += 1
                P.emit("tensor", lambda e, fc=fc, hh=hh: e.matmul(
                    psbv[:, 512 * hh:512 * (hh + 1)], bvcol_bf[:, fc:fc + 1],
                    wbuf1[:, DM * fc + 512 * hh: DM * fc + 512 * (hh + 1)],
                    start=(fc == 0), stop=(fc == NFC - 1)),
                    waits=([("dwb1", P.val("dwb1")), ("dsg", V_DSG)] if n == 1 else []),
                    inc=("t1" if n >= 15 else None))
        P.emit("sync", lambda e: e.dma_start(rowB[:, :], bo_h.ap()[:, :]),
               waits=[("t1", T1_BETA)], inc="dsm", inc_n=16)
        tv = P.val("t1")
        P.emit("vector", lambda e: e.tensor_add(rowB[:, 0:DM], rowB[:, 0:DM], psbv[:, :]),
               waits=[("t1", tv), ("dsm", 96)], writes=("rowBB",))
        P.emit("vector", lambda e: e.tensor_copy(borow_bf[:, :], rowB[:, 0:DM]),
               reads=("rowBB",), inc="v1")
        emit_ln1_pair(0)
        emit_h0(0)
        P.flush_block()

    slot_free = {0: None, 1: None}
    wload_v = {}

    def load_weight(oct_, widx):
        slot = widx % 2
        dwbs = "dwb%d" % slot
        tw = [("dwh0", 32), ("dwh1", 32)]
        if slot_free[slot] is not None:
            tw.append(slot_free[slot])
        P.emit("gpsimd", lambda e, widx=widx, slot=slot: e.dma_start(
            wbuf[slot][:, :], wbh[widx].ap()[:, :]),
            waits=tw, inc=dwbs, inc_n=16)
        wload_v[(oct_, widx)] = (dwbs, P.val(dwbs))

    PLV = {}
    H2V = {}
    TAIL_EV = {}
    T1_HT = {}
    TAILT1 = {}
    CHST = {}

    def chain_stage(co, stage):
        st = CHST.setdefault(co, {"a2": {}, "r2": {}, "a3": {}, "r3": {}})
        pv = lambda i: preLN[:, DM * i:DM * (i + 1)]
        xv = lambda i: xnscr[:, DM * i:DM * (i + 1)]
        hv = lambda i: hf32[:, DM * i:DM * (i + 1)]
        c = lambda i, idx: stats[:, LNW * i + idx:LNW * i + idx + 1]
        if stage == 0:
            for i in range(2):
                P.emit("vector", lambda e, i=i: e.bn_stats(
                    stats[:, LNW * i:LNW * i + 6], pv(i)[:, 0:512]),
                    waits=[("v1", PLV[(co, i)])], reads=("pl%d" % i,),
                    writes=("c%db0" % i,))
                P.emit("vector", lambda e, i=i: e.bn_stats(
                    stats[:, LNW * i + 6:LNW * i + 12], pv(i)[:, 512:1024]),
                    reads=("pl%d" % i,), writes=("c%db1" % i,))
                P.emit("vector", lambda e, i=i: e.bn_aggr(
                    stats[:, LNW * i + 12:LNW * i + 14], stats[:, LNW * i:LNW * i + 12]),
                    reads=("c%db0" % i, "c%db1" % i), inc="v2")
                st["a2"][i] = P.val("v2")
        elif stage == 1:
            for i in range(2):
                P.emit("scalar", lambda e, i=i: e.activation(
                    c(i, 14), c(i, 13), AF.Ln, bias=ceps[:, :]),
                    waits=[("v2", st["a2"][i])], writes=("crs%d" % i,))
                P.emit("scalar", lambda e, i=i: e.activation(
                    c(i, 15), c(i, 14), AF.Exp, scale=cmhalf[:, :]),
                    reads=("crs%d" % i,), inc="s2")
                st["r2"][i] = P.val("s2")
        elif stage == 2:
            for i in range(2):
                tt = 2 * co + i
                P.emit("vector", lambda e, i=i: e.scalar_tensor_tensor(
                    c(i, 16), c(i, 12), -1.0, c(i, 15), ALU.mult, ALU.mult),
                    waits=[("s2", st["r2"][i])], writes=("cnm%d" % i,))
                P.emit("vector", lambda e, i=i: e.tensor_scalar(
                    xv(i), pv(i), c(i, 15), c(i, 16), ALU.mult, ALU.add),
                    reads=("cnm%d" % i, "pl%d" % i), writes=("cxn%d" % i,))
                P.emit("vector", lambda e, i=i: e.tensor_mul(
                    pv(i), xv(i), GAMGA[:, :]),
                    reads=("cxn%d" % i,), writes=("pl%d" % i,))
                P.emit("vector", lambda e, i=i: e.tensor_add(
                    pv(i), pv(i), BETGA[:, :]),
                    reads=("pl%d" % i,), writes=("pl%d" % i,))
                P.emit("vector", lambda e, i=i, tt=tt: e.tensor_add(
                    xsb[:, DM * tt:DM * (tt + 1)], xsb[:, DM * tt:DM * (tt + 1)],
                    pv(i)),
                    reads=("pl%d" % i,), writes=("x%d" % tt,), inc="v1")
                P.emit("vector", lambda e, i=i, tt=tt: e.bn_stats(
                    stats[:, LNW * i:LNW * i + 6], xsb[:, DM * tt:DM * tt + 512]),
                    reads=("x%d" % tt,), writes=("c%db0" % i,))
                P.emit("vector", lambda e, i=i, tt=tt: e.bn_stats(
                    stats[:, LNW * i + 6:LNW * i + 12],
                    xsb[:, DM * tt + 512:DM * tt + 1024]),
                    reads=("x%d" % tt,), writes=("c%db1" % i,))
                P.emit("vector", lambda e, i=i: e.bn_aggr(
                    stats[:, LNW * i + 12:LNW * i + 14], stats[:, LNW * i:LNW * i + 12]),
                    reads=("c%db0" % i, "c%db1" % i), inc="v2")
                st["a3"][i] = P.val("v2")
        elif stage == 3:
            for i in range(2):
                P.emit("scalar", lambda e, i=i: e.activation(
                    c(i, 14), c(i, 13), AF.Ln, bias=ceps[:, :]),
                    waits=[("v2", st["a3"][i])], writes=("crs%d" % i,))
                P.emit("scalar", lambda e, i=i: e.activation(
                    c(i, 15), c(i, 14), AF.Exp, scale=cmhalf[:, :]),
                    reads=("crs%d" % i,), inc="s2")
                st["r3"][i] = P.val("s2")
        elif stage == 4:
            t1w = T1_HT[min(co + 1, NOCT - 1)]
            for i in range(2):
                tt = 2 * co + i
                P.emit("vector", lambda e, i=i: e.scalar_tensor_tensor(
                    c(i, 16), c(i, 12), -1.0, c(i, 15), ALU.mult, ALU.mult),
                    waits=[("s2", st["r3"][i])], writes=("cnm%d" % i,))
                P.emit("vector", lambda e, i=i, tt=tt: e.tensor_scalar(
                    xv(i), xsb[:, DM * tt:DM * (tt + 1)], c(i, 15), c(i, 16),
                    ALU.mult, ALU.add),
                    reads=("cnm%d" % i, "x%d" % tt), writes=("cxn%d" % i,))
            emit_h2(co, 0, ("t1", t1w))

    def emit_h2(co, i, t1wait):
        xv = lambda i: xnscr[:, DM * i:DM * (i + 1)]
        P.emit("vector", lambda e, i=i: e.tensor_mul(
            hf32[:, 0:DM], xv(i), CBb[:, :]),
            waits=[t1wait],
            reads=("cxn%d" % i,), writes=("ch2",))
        P.emit("vector", lambda e: e.tensor_add(
            hf32[:, 0:DM], hf32[:, 0:DM], SBb[:, :]),
            reads=("ch2",), writes=("ch2",), inc="v1")
        H2V[(co, i)] = P.val("v1")

    def emit_tail_transposes(co, rc_guard):
        pstcB = (pbD[:, 0:128], pbD[:, 512:640])
        tail_ev = {}
        tail_t1 = {}
        for gp in range(16):
            i_t, cch, tpar = gp // 8, gp % 8, gp % 2
            fw = []
            if gp == 8:
                emit_h2(co, 1, ("t1", tail_t1[7]))
            if gp in (0, 8):
                fw.append(("v1", H2V[(co, i_t)]))
            if gp == 0:
                fw.append(("v1", rc_guard))
            if gp >= 2:
                fw.append(("v1", tail_ev[gp - 2]))
            P.emit("tensor", lambda e, cch=cch, tpar=tpar: e.transpose(
                pstcB[tpar][:, :],
                hf32[:, 128 * cch:128 * (cch + 1)],
                identf[:, :]), waits=fw, inc="t1")
            ttv = P.val("t1")
            tail_t1[gp] = ttv
            tt_prev = 2 * co + i_t
            P.emit("vector", lambda e, cch=cch, tpar=tpar, tt_prev=tt_prev: e.tensor_copy(
                h2T[:, T * cch + 128 * tt_prev:T * cch + 128 * (tt_prev + 1)],
                pstcB[tpar][:, :]),
                waits=[("t1", ttv)], reads=("ch2",), inc="v1")
            tail_ev[gp] = P.val("v1")
        TAIL_EV[co] = tail_ev[15]
        TAILT1[co] = tail_t1[15]

    def emit_gate_tail(co):
        psgt = pbD[:, 256:256 + NEXP]
        psg8 = pbC[0:NEXP, 0:128]
        for i in range(2):
            tt = 2 * co + i
            for fc in range(NFC):
                gwt = []
                if fc == 0:
                    gwt = [("v1", TAIL_EV[co] if i == 0 else P.val("v1")),
                           ("dsg", V_DSG)]
                P.emit("tensor", lambda e, tt=tt, fc=fc: e.matmul(
                    psg8[:, :], gw_bf[:, NEXP * fc:NEXP * (fc + 1)],
                    h2T[:, T * fc + 128 * tt:T * fc + 128 * (tt + 1)],
                    start=(fc == 0), stop=(fc == NFC - 1)),
                    waits=gwt,
                    inc=("m2" if fc == NFC - 1 else None))
            tv = P.val("m2")
            P.emit("vector", lambda e: e.tensor_copy(gEM[:, :], psg8[:, :]),
                   waits=[("m2", tv)], writes=("gEM",), inc="v1")
            vv = P.val("v1")
            P.emit("tensor", lambda e: e.transpose(
                psgt[:, :], gEM[0:NEXP, :], identf[0:NEXP, 0:NEXP]),
                waits=[("v1", vv)], inc="t1")
            tv = P.val("t1")
            c = lambda nm, i=i: stats[:, G0 + GC[nm] + i:G0 + GC[nm] + i + 1]
            P.emit("vector", lambda e: e.tensor_add(gatef[:, :], psgt[:, :], GBIASb[:, :]),
                   waits=[("t1", tv)], writes=("gatef",))
            P.emit("vector", lambda e, i=i: e.reduce_max(c("gmax", i), gatef[:, :], axis=AX.X),
                   reads=("gatef",), writes=("gmax",))
            P.emit("vector", lambda e, i=i: e.tensor_scalar(
                c("gnm", i), c("gmax", i), -1.0, None, ALU.mult),
                reads=("gmax",), inc="v2")
            vv = P.val("v2")
            P.emit("scalar", lambda e, i=i: e.activation(
                gatee[:, :], gatef[:, :], AF.Exp, bias=c("gnm", i)),
                waits=[("v2", vv)], inc="s2")
            sv = P.val("s2")
            P.emit("vector", lambda e, i=i: e.reduce_sum(c("gsum", i), gatee[:, :], axis=AX.X),
                   waits=[("s2", sv)], writes=("gsum",))
            P.emit("vector", lambda e, i=i: e.reciprocal(c("grec", i), c("gsum", i)),
                   reads=("gsum",), writes=("grec",))
            P.emit("vector", lambda e, i=i: e.tensor_scalar(
                gatee[:, :], gatee[:, :], c("grec", i), None, ALU.mult),
                reads=("grec",), writes=("probs",))
            P.emit("vector", lambda e: e.tensor_scalar(
                gatef[:, :], gatee[:, :], 0.1, None, ALU.is_ge),
                reads=("probs",), writes=("sel",))
            P.emit("vector", lambda e: e.tensor_mul(gatee[:, :], gatee[:, :], gatef[:, :]),
                   reads=("sel",), writes=("wsel",))
            P.emit("vector", lambda e, i=i: e.reduce_sum(c("gws", i), gatee[:, :], axis=AX.X),
                   reads=("wsel",), writes=("gws",))
            P.emit("vector", lambda e, i=i: e.tensor_scalar(
                c("gws", i), c("gws", i), 0.05, None, ALU.max),
                reads=("gws",), writes=("gws",))
            P.emit("vector", lambda e, i=i: e.reciprocal(c("gwr", i), c("gws", i)),
                   reads=("gws",), writes=("gwr",))
            P.emit("vector", lambda e, i=i, tt=tt: e.tensor_scalar(
                w_gate[:, NEXP * tt:NEXP * (tt + 1)], gatee[:, :],
                c("gwr", i), None, ALU.mult),
                reads=("gwr",), writes=("wg%d" % tt,), inc="v1")
            vv = P.val("v1")
            P.emit("tensor", lambda e, tt=tt: e.transpose(
                psg8[:, :], w_gate[:, NEXP * tt:NEXP * (tt + 1)], identf[:, :]),
                waits=[("v1", vv)], inc="t1")
            tv = P.val("t1")
            P.emit("vector", lambda e, tt=tt: e.tensor_copy(
                wT_bf[:, 128 * tt:128 * (tt + 1)], psg8[:, :]),
                waits=[("t1", tv)], inc="v1")

    pbd_guard = 0
    for oct_ in range(NOCT):
        if True:
            psq0 = pbA[:, 0:256]
            psq1 = pbA[:, 512:768]
            psv0 = pbB[:, :]
            psv1 = pbC[:, :]
            pst0 = pbD[:, 0:128]
            pst1 = pbD[:, 512:640]
            psq = (psq0, psq1)
            psv = (psv0, psv1)
            pst = (pst0, pst1)

            if oct_ == 0:
                load_weight(0, 0)
                load_weight(0, 1)

            P.emit("vector", lambda e: e.memset(vview[:, :, :, 64:65], 1.0))
            for i in range(2):
                tt = 2 * oct_ + i
                if i == 1:
                    hw = [ln1_sv[oct_][1], ("t1", P.val("t1"))]
                    P.emit("vector", lambda e: e.tensor_mul(
                        hf32[:, 0:DM], Kasm[:, DM:2 * DM], CAb[:, :]),
                        waits=hw, writes=("hf",))
                    P.emit("vector", lambda e: e.tensor_add(
                        hf32[:, 0:DM], hf32[:, 0:DM], SAb[:, :]),
                        reads=("hf",), writes=("hf",), inc="v1")
                    P.emit("vector", lambda e: e.tensor_copy(
                        h_bf[:, DM:2 * DM], hf32[:, 0:DM]),
                        reads=("hf",), inc="v1")
                    hdone_i = P.val("v1")
                else:
                    hdone_i = HD0[oct_]
                ev0 = P.val("v1")
                for cch in range(NFC):
                    k = i * NFC + cch
                    par = k % 2
                    tw = [("v1", hdone_i)]
                    if cch >= 2:
                        tw.append(("v1", ev0 + cch - 1))
                    elif i == 1:
                        tw.append(("v1", ev0 + cch - 1))
                    P.emit("tensor", lambda e, cch=cch, par=par: e.transpose(
                        pst[par][:, :],
                        hf32[:, 128 * cch:128 * (cch + 1)],
                        identf[:, :]), waits=tw, inc="t1")
                    tv = P.val("t1")
                    P.emit("vector", lambda e, i=i, cch=cch, par=par: e.tensor_copy(
                        hT[:, 256 * cch + 128 * i:256 * cch + 128 * (i + 1)],
                        pst[par][:, :]),
                        waits=[("t1", tv)], inc="v1")
            V_HT = P.val("v1")
            T1_HT[oct_] = P.val("t1")

            evac_after = {}
            gidx = 0
            for widx, (bcol, dst) in enumerate(((bqcol, Qasm), (bkcol, Kasm))):
                wtile = wbuf[widx % 2]
                dv = qview(dst)
                for fo in range(NFC):
                    par = gidx % 2
                    for fc in range(NFC):
                        tw = []
                        if fc == 0:
                            if fo == 0:
                                tw += [("v1", V_HT), wload_v[(oct_, widx)]]
                                if widx == 0 and oct_ >= 1:
                                    tw.append(("v1", PLV[(oct_ - 1, 0)]))
                            if gidx >= 2:
                                tw.append(("v1", evac_after[gidx - 2]))
                        P.emit("tensor", lambda e, wtile=wtile, fo=fo, fc=fc, par=par: e.matmul(
                            psq[par][:, :],
                            wtile[:, DM * fc + 128 * fo:DM * fc + 128 * (fo + 1)],
                            hT[:, 256 * fc:256 * (fc + 1)],
                            start=(fc == 0), stop=(fc == NFC - 1)),
                            waits=tw, inc=("m1" if fc == NFC - 1 else None))
                    tv = P.val("m1")
                    evw = [("m1", tv)]
                    if oct_ == 0 and fo == 0:
                        evw.append(("dsm", V_DSM))
                    P.emit("vector", lambda e, dv=dv, fo=fo, par=par, bcol=bcol: e.tensor_scalar(
                        dv[0:64, :, 2 * fo, :],
                        psq[par][0:64, :].rearrange("p (h s) -> p h s", h=2),
                        bcol[0:64, fo:fo + 1], None, ALU.add),
                        waits=evw)
                    P.emit("vector", lambda e, dv=dv, fo=fo, par=par, bcol=bcol: e.tensor_scalar(
                        dv[64:128, :, 2 * fo + 1, :],
                        psq[par][64:128, :].rearrange("p (h s) -> p h s", h=2),
                        bcol[64:128, fo:fo + 1], None, ALU.add),
                        inc="v1")
                    evac_after[gidx] = P.val("v1")
                    gidx += 1
                if widx == 0:
                    slot_free[0] = ("m1", P.val("m1"))
                    load_weight(oct_, 2)
                    dv4 = qview4(Qasm)
                    P.emit("gpsimd", lambda e, dv4=dv4: e.dma_start(
                        dv4[0:64, :, :, 1, :], dv4[64:128, :, :, 1, :]),
                        waits=[("v1", P.val("v1")), ("dsh", P.val("dsh"))],
                        inc="dsh", inc_n=16)
                    V_DSH_Q = P.val("dsh")
                else:
                    slot_free[1] = ("m1", P.val("m1"))
                    load_weight(oct_, 3)
            V_QK = P.val("v1")

            dv4 = qview4(Kasm)
            P.emit("sync", lambda e, dv4=dv4: e.dma_start(
                dv4[0:64, :, :, 1, :], dv4[64:128, :, :, 1, :]),
                waits=[("v1", V_QK)], inc="dsm", inc_n=16)
            V_DSM_K = P.val("dsm")

            for i in range(2):
                for fc in range(NFC):
                    for hh in range(2):
                        tw = []
                        if fc == 0 and hh == 0:
                            tw = [("v1", V_HT), wload_v[(oct_, 2)]]
                            if i == 0 and oct_ >= 1:
                                tw.append(("v1", PLV[(oct_ - 1, 1)]))
                        P.emit("tensor", lambda e, i=i, fc=fc, hh=hh: e.matmul(
                            psv[i][:, 512 * hh:512 * (hh + 1)],
                            hT[:, 256 * fc + 128 * i:256 * fc + 128 * (i + 1)],
                            wbuf0[:, DM * fc + 512 * hh:DM * fc + 512 * (hh + 1)],
                            start=(fc == 0), stop=(fc == NFC - 1)),
                            waits=tw,
                            inc=("m1" if fc == NFC - 1 else None))
                tv = P.val("m1")
                P.emit("vector", lambda e, i=i: e.tensor_copy(
                    vview[:, i, :, 0:64],
                    psv[i][:, :].rearrange("p (g d) -> p g d", g=16)),
                    waits=[("m1", tv)], inc="v1")
                if i == 0:
                    V_PSV0 = P.val("v1")
            V_PSV1 = P.val("v1")
            slot_free[0] = ("m1", P.val("m1"))

            for wu in range(16):
                P.emit("tensor", lambda e: e.matmul(
                    pbD[:, 0:512], CAb[:, 0:128], CAb[:, 0:512],
                    start=True, stop=True),
                    inc=("m1" if wu == 15 else None))

        if True:
            pss0 = pbA[:, :]
            pss1 = pbB[:, :]
            psc = pbC[0:65, :]
            psr = pbD[0:64, :]
            lnrow = recipB
            pss = (pss0, pss1)
            exp_done = {}
            mm2_done = {}
            evac_done = 0
            recip_copied = pbd_guard
            tail_on = oct_ >= 1
            co = oct_ - 1
            tail_ev = {}
            tail_t1 = {}
            pstcB = (pbD[:, 0:128], pbD[:, 512:640])

            def emit_mm2(k, seg):
                h01, gp = k // 32, (k % 16)
                par = k % 2
                for q4 in range(2):
                    tw = [("s1", exp_done[k])] if q4 == 0 else []
                    if gp == 0 and q4 == 0:
                        tw.append(("v1", evac_done if seg > 0 else V_PSV1))
                    P.emit("tensor", lambda e, h01=h01, gp=gp, q4=q4, par=par: e.matmul(
                        psc[:, 512 * q4:512 * (q4 + 1)],
                        vview[:, h01, gp, 0:65],
                        PT[:, 1024 * par + 512 * q4:1024 * par + 512 * (q4 + 1)],
                        start=(gp == 0), stop=(gp == 15)),
                        waits=tw,
                        inc=("t2" if (q4 == 1 or gp == 15) else None))
                mm2_done[k] = P.val("t2")

            pend = None
            for seg in range(4):
                h01, q2 = seg // 2, seg % 2
                if tail_on and seg < 3:
                    chain_stage(co, (0, 2, 4)[seg])
                for gp in range(16):
                    k = seg * 16 + gp
                    par = k % 2
                    for q4 in range(2):
                        tw = []
                        if q4 == 0:
                            if k == 0:
                                tw.append(("dsh", V_DSH_Q))
                                tw.append(("dsm", V_DSM_K))
                            if k == 1:
                                tw.append(("v1", V_PSV0))
                            if k >= 2:
                                tw.append(("s1", exp_done[k - 2]))
                        P.emit("tensor", lambda e, h01=h01, q2=q2, gp=gp, q4=q4, par=par: e.matmul(
                            pss[par][:, 512 * q4:512 * (q4 + 1)],
                            Kasm[0:64, T * h01 + 128 * gp:T * h01 + 128 * (gp + 1)],
                            Qasm[0:64, T * h01 + 1024 * q2 + 512 * q4:
                                 T * h01 + 1024 * q2 + 512 * (q4 + 1)],
                            start=True, stop=True),
                            waits=tw, inc="m1")
                    mm1v = P.val("m1")
                    tw = [("m1", mm1v)]
                    if k >= 2:
                        tw.append(("t2", mm2_done[k - 2]))
                    P.emit("scalar", lambda e, par=par: e.activation(
                        PT[:, 1024 * par:1024 * (par + 1)], pss[par][:, :], AF.Exp,
                        bias=cm3[:, :], scale=c0125[:, :]),
                        waits=tw, inc="s1")
                    exp_done[k] = P.val("s1")
                    if tail_on and gp == 12 and seg <= 1:
                        chain_stage(co, (1, 3)[seg])
                    fw = [("v1", recip_copied)] if gp == 0 else []
                    P.emit("tensor", lambda e: e.matmul(
                        pbD[:, 0:128], CAb[:, 0:128], CAb[:, 0:128],
                        start=True, stop=True), waits=fw,
                        inc=("m1" if gp == 15 else None))
                    if pend is not None:
                        emit_mm2(*pend)
                    pend = (k, seg)
                emit_mm2(*pend)
                pend = None
                for fb in range(8):
                    P.emit("tensor", lambda e: e.matmul(
                        pbD[:, 0:512], CAb[:, 0:128], CAb[:, 0:512],
                        start=True, stop=True),
                        inc=("m1" if fb == 7 else None))

                tv2 = mm2_done[seg * 16 + 15]
                P.emit("scalar", lambda e: e.activation(
                    lnrow[96:97, :], psc[64:65, :], AF.Ln),
                    waits=[("t2", tv2)], writes=("lnrow",))
                P.emit("scalar", lambda e: e.activation(
                    recrow[64:65, :], lnrow[96:97, :], AF.Exp, scale=cn1[64:65, :]),
                    reads=("lnrow",), inc="s2")
                sv = P.val("s2")
                for hh in range(2):
                    tw = []
                    if hh == 0:
                        tw.append(("s2", sv))
                        if seg > 0:
                            tw.append(("v1", recip_copied))
                    P.emit("tensor", lambda e, hh=hh: e.matmul(
                        psr[:, 512 * hh:512 * (hh + 1)], onesb[64:65, :],
                        recrow[64:65, 512 * hh:512 * (hh + 1)], start=True, stop=True),
                        waits=tw, inc="t2")
                tv = P.val("t2")
                P.emit("vector", lambda e: e.tensor_copy(recipB[0:64, :], psr[:, :]),
                       waits=[("t2", tv)], writes=("recipB",), inc="v1")
                recip_copied = P.val("v1")
                pcv = psc[0:64, :].rearrange("p (gb pa s) -> p gb pa s", gb=4, pa=2, s=128)
                rbv = recipB[0:64, :].rearrange("p (gb pa s) -> p gb pa s", gb=4, pa=2, s=128)
                P.emit("vector", lambda e, q2=q2, h01=h01, pcv=pcv, rbv=rbv: e.tensor_tensor(
                    cview[0:64, 0, 4 * q2:4 * (q2 + 1), h01, :],
                    pcv[:, :, 0, :], rbv[:, :, 0, :], ALU.mult),
                    reads=("recipB",))
                P.emit("vector", lambda e, q2=q2, h01=h01, pcv=pcv, rbv=rbv: e.tensor_tensor(
                    cview[0:64, 1, 4 * q2:4 * (q2 + 1), h01, :],
                    pcv[:, :, 1, :], rbv[:, :, 1, :], ALU.mult),
                    inc="v1")
                evac_done = P.val("v1")

            P.emit("gpsimd", lambda e: e.dma_start(
                cview[64:128, 0, :, :, :], cview[0:64, 1, :, :, :]),
                waits=[("v1", evac_done), ("dsh", P.val("dsh"))], inc="dsh", inc_n=16)
            V_DSH_CTX = P.val("dsh")
            V_EXP_LAST = exp_done[63]

        if True:
            if oct_ >= 1:
                emit_tail_transposes(oct_ - 1, recip_copied)
            if oct_ + 1 < NOCT:
                emit_ln1_pair(oct_ + 1)
                emit_h0(oct_ + 1,
                        extra_t1=(TAILT1[oct_ - 1] if oct_ >= 1 else None))
            pso0 = pbA[:, :]
            pso1 = pbB[:, :]
            pso = (pso0, pso1)
            m2v = {}
            for i in range(2):
                for ec in range(NFC):
                    for hh in range(2):
                        tw = []
                        if i == 0 and ec == 0 and hh == 0:
                            tw = [("dsh", V_DSH_CTX), ("s1", V_EXP_LAST),
                                  wload_v[(oct_, 3)]]
                        P.emit("tensor", lambda e, i=i, ec=ec, hh=hh: e.matmul(
                            pso[i][:, 512 * hh:512 * (hh + 1)],
                            cview[:, 0, ec, i, :],
                            wbuf1[:, DM * ec + 512 * hh:DM * ec + 512 * (hh + 1)],
                            start=(ec == 0), stop=False),
                            waits=tw, inc=None)
                for hh in range(2):
                    P.emit("tensor", lambda e, i=i, hh=hh: e.matmul(
                        pso[i][:, 512 * hh:512 * (hh + 1)], ones1b[:, :],
                        borow_bf[:, 512 * hh:512 * (hh + 1)],
                        start=False, stop=True),
                        inc="m2")
                m2v[i] = P.val("m2")
            slot_free[1] = ("m2", P.val("m2"))
            for i in range(2):
                hcol = DM if i == 1 else h0slot(oct_)
                P.emit("vector", lambda e, i=i, hcol=hcol: e.tensor_add(
                    preLN[:, DM * i:DM * (i + 1)], pso[i][:, :],
                    h_bf[:, hcol:hcol + DM]),
                    waits=[("m2", m2v[i])], writes=("pl%d" % i,), inc="v1")
                PLV[(oct_, i)] = P.val("v1")
            if oct_ >= 1:
                emit_gate_tail(oct_ - 1)
            if oct_ + 1 < NOCT:
                load_weight(oct_ + 1, 0)
                load_weight(oct_ + 1, 1)
            pbd_guard = P.val("v1")

    if True:
        for s in range(5):
            chain_stage(NOCT - 1, s)
        emit_tail_transposes(NOCT - 1, 0)
        emit_gate_tail(NOCT - 1)

    es2.close()
    with nc.sbuf_tensor("We0", [128, NFC * DM], BF16) as We0, \
         nc.sbuf_tensor("We1", [128, NFC * DM], BF16) as We1, \
         nc.sbuf_tensor("facc", [128, NB * DM], BF16) as facc, \
         nc.sbuf_tensor("ufin", [128, DM], BF16) as ufin, \
         nc.sbuf_tensor("eb_bf", [NEXP, DM], BF16) as eb_bf:
        pse0 = pbA[:, :]
        pse1 = pbB[:, :]
        P.emit("gpsimd", lambda e: e.dma_start(eb_bf[:, :], eb_h.ap()[:, :]),
               waits=[("m2", P.val("m2"))], inc="dsg", inc_n=16)
        V_EB = P.val("dsg")
        We = (We0, We1)
        pse = (pse0, pse1)
        V_P5START = P.val("v1")
        M2_P5START = P.val("m2")

        binit = {}
        for tt in range(NB):
            par = tt % 2
            for hh in range(2):
                tw = []
                if hh == 0 and tt >= 2:
                    tw.append(("v1", binit[tt - 2]))
                if hh == 0 and tt == 0:
                    tw.append(("dsg", V_EB))
                    tw.append(("v1", V_P5START))
                P.emit("tensor", lambda e, tt=tt, hh=hh, par=par: e.matmul(
                    pse[par][:, 512 * hh:512 * (hh + 1)],
                    wT_bf[:, 128 * tt:128 * (tt + 1)],
                    eb_bf[:, 512 * hh:512 * (hh + 1)], start=True, stop=True),
                    waits=tw, inc="m2")
            tv = P.val("m2")
            P.emit("vector", lambda e, tt=tt, par=par: e.tensor_copy(
                facc[:, DM * tt:DM * (tt + 1)], pse[par][:, :]),
                waits=[("m2", tv)], inc="v1", writes=("fa%d" % tt,))
            binit[tt] = P.val("v1")
        V_BINIT = P.val("v1")

        grp_done = {}
        stt_done = {}
        for e_i in range(NEXP):
            tw = [("m2", grp_done[(e_i - 2, NB - 1)] if e_i >= 2 else M2_P5START)]
            P.emit("gpsimd", lambda e, e_i=e_i: e.dma_start(
                We[e_i % 2][:, :].rearrange("p (f o) -> p f o", f=NFC),
                ew_h.ap()[e_i:e_i + 1, :, :].rearrange(
                    "a (f p) o -> p (a f) o", p=128)),
                waits=tw, inc="dew%d" % (e_i % 2), inc_n=16)
            for tt in range(NB):
                par = tt % 2
                k = e_i * NB + tt
                for fc in range(NFC):
                    for hh in range(2):
                        tw = []
                        if fc == 0 and hh == 0:
                            tw.append(("dew%d" % (e_i % 2), 16 * (e_i // 2 + 1)))
                            if e_i == 0:
                                tw.append(("v1", V_BINIT))
                            if k >= 2:
                                tw.append(("v1", stt_done[k - 2]))
                        P.emit("tensor", lambda e, e_i=e_i, tt=tt, fc=fc, hh=hh, par=par: e.matmul(
                            pse[par][:, 512 * hh:512 * (hh + 1)],
                            h2T[:, T * fc + 128 * tt:T * fc + 128 * (tt + 1)],
                            We[e_i % 2][:, DM * fc + 512 * hh:DM * fc + 512 * (hh + 1)],
                            start=(fc == 0), stop=(fc == NFC - 1)),
                            waits=tw,
                            inc=("m2" if fc == NFC - 1 else None))
                grp_done[(e_i, tt)] = P.val("m2")
                P.emit("vector", lambda e, e_i=e_i, tt=tt, par=par: e.scalar_tensor_tensor(
                    facc[:, DM * tt:DM * (tt + 1)], pse[par][:, :],
                    w_gate[:, NEXP * tt + e_i:NEXP * tt + e_i + 1],
                    facc[:, DM * tt:DM * (tt + 1)], ALU.mult, ALU.add),
                    waits=[("m2", grp_done[(e_i, tt)])], inc="v1",
                    reads=("fa%d" % tt,), writes=("fa%d" % tt,))
                stt_done[k] = P.val("v1")

        for tt in range(NB):
            P.emit("vector", lambda e, tt=tt: e.tensor_mul(
                ufin[:, :], facc[:, DM * tt:DM * (tt + 1)], GBb[:, :]),
                reads=("fa%d" % tt,), writes=("ufin",))
            P.emit("vector", lambda e, tt=tt: e.tensor_add(
                xsb[:, DM * tt:DM * (tt + 1)], xsb[:, DM * tt:DM * (tt + 1)],
                ufin[:, :]),
                reads=("ufin",), inc="v1")
            vv = P.val("v1")
            P.emit("sync", lambda e, tt=tt: e.dma_start(
                out_h.ap()[128 * tt:128 * (tt + 1), :], xsb[:, DM * tt:DM * (tt + 1)]),
                waits=[("v1", vv)], inc="dout", inc_n=16)
        P.emit("sync", lambda e: e.nop(), waits=[("dout", P.val("dout"))])
        P.flush_block()

    es.close()
    return nc


_NC_CACHE = {}


def kernel(L0_attn_x, L0_attn_wq, L0_attn_bq, L0_attn_wk, L0_attn_bk,
           L0_attn_wv, L0_attn_bv, L0_attn_wo, L0_attn_bo, L0_attn_gamma,
           L0_attn_beta, L0_ffn_gate_w, L0_ffn_gate_b, L0_ffn_expert_w,
           L0_ffn_expert_b, L0_c, L0_wmod):
    if "nc" not in _NC_CACHE:
        _NC_CACHE["nc"] = build_program()
    nc = _NC_CACHE["nc"]
    f32 = lambda a: np.ascontiguousarray(np.asarray(a), dtype=np.float32)
    common = {
        "wq": f32(L0_attn_wq), "wk": f32(L0_attn_wk),
        "wv": f32(L0_attn_wv), "wo": f32(L0_attn_wo),
        "bq": f32(L0_attn_bq).reshape(1, DM), "bk": f32(L0_attn_bk).reshape(1, DM),
        "bv": f32(L0_attn_bv).reshape(1, DM), "bo": f32(L0_attn_bo).reshape(1, DM),
        "gamma": f32(L0_attn_gamma).reshape(1, DM),
        "beta": f32(L0_attn_beta).reshape(1, DM),
        "gw": f32(L0_ffn_gate_w), "gb": f32(L0_ffn_gate_b).reshape(1, NEXP),
        "eb": f32(L0_ffn_expert_b), "ew": f32(L0_ffn_expert_w),
        "wmod": f32(L0_wmod),
    }
    xs = f32(L0_attn_x)
    cs = f32(L0_c)
    in_maps = [dict(common, x=xs[b], c=cs[b:b + 1]) for b in range(NCORE)]
    res = run_bass_kernel_spmd(nc, in_maps, core_ids=list(range(NCORE)))
    return np.stack([res.results[b]["out"] for b in range(NCORE)], axis=0)


# version 12 (final)
# speedup vs baseline: 1.0597x; 1.0597x over previous
import numpy as np
from contextlib import ExitStack

try:
    import concourse.bass as bass
except ImportError:
    import sys
    sys.path.insert(0, "/opt/trn_rl_repo")
    import concourse.bass as bass
import concourse.mybir as mybir
from concourse.bass_utils import run_bass_kernel_spmd

F32 = mybir.dt.float32
BF16 = mybir.dt.bfloat16
AF = mybir.ActivationFunctionType
ALU = mybir.AluOpType
AX = mybir.AxisListType

NCORE = 8
T = 2048
DM = 1024
NB = 16
NOCT = 8
NFC = 8
NEXP = 8
EPS = 1e-5
INV_DM = 1.0 / DM


class Prog:

    ENGS = ("tensor", "vector", "scalar", "gpsimd", "sync")

    def __init__(self, nc, sems, gap_dummy):
        self.nc = nc
        self.sems = sems
        self.count = {k: 0 for k in sems}
        self.gap = gap_dummy
        self._reset()

    def _reset(self):
        self.ops = {e: [] for e in self.ENGS}
        self.written = {e: set() for e in self.ENGS}
        self.readk = {e: set() for e in self.ENGS}
        self.dirty = {e: False for e in self.ENGS}

    def emit(self, eng, fn, waits=(), inc=None, inc_n=1, reads=(), writes=()):
        if eng in ("vector", "scalar"):
            need = (any(r in self.written[eng] for r in reads)
                    or any(w in self.readk[eng] or w in self.written[eng]
                           for w in writes)
                    or inc is not None)
            if need:
                self.ops[eng].append((lambda e: e.drain(), (), None, 0))
                self.written[eng].clear()
                self.readk[eng].clear()
                self.dirty[eng] = False
        self.ops[eng].append((fn, tuple(waits), inc, inc_n))
        if inc is not None:
            self.count[inc] += inc_n
        elif eng in ("vector", "scalar"):
            self.dirty[eng] = True
        for wkey in writes:
            self.written[eng].add(wkey)
        for rkey in reads:
            self.readk[eng].add(rkey)
        return self.count[inc] if inc else None

    def val(self, sem):
        return self.count[sem]

    def flush_block(self):
        ops, sems = self.ops, self.sems

        def mk(eng_name):
            def run(eng):
                for fn, waits, inc, inc_n in ops[eng_name]:
                    for s, v in waits:
                        eng.wait_ge(sems[s], v)
                    ins = fn(eng)
                    if inc is not None:
                        ins.then_inc(sems[inc], inc_n)
            return run

        with self.nc.Block() as blk:
            blk.tensor(mk("tensor"))
            blk.vector(mk("vector"))
            blk.scalar(mk("scalar"))
            blk.gpsimd(mk("gpsimd"))
            blk.sync(mk("sync"))
        self._reset()


LNW = 20
G0 = 80
GC = {"gmax": 0, "gnm": 2, "gsum": 4, "grec": 6, "gws": 8, "gwr": 10}
PIDX = 94


def build_program():
    nc = bass.Bass("TRN2", target_bir_lowering=False)

    x_h = nc.dram_tensor("x", [T, DM], F32, kind="ExternalInput")
    wq_h = nc.dram_tensor("wq", [DM, DM], F32, kind="ExternalInput")
    wk_h = nc.dram_tensor("wk", [DM, DM], F32, kind="ExternalInput")
    wv_h = nc.dram_tensor("wv", [DM, DM], F32, kind="ExternalInput")
    wo_h = nc.dram_tensor("wo", [DM, DM], F32, kind="ExternalInput")
    bq_h = nc.dram_tensor("bq", [1, DM], F32, kind="ExternalInput")
    bk_h = nc.dram_tensor("bk", [1, DM], F32, kind="ExternalInput")
    bv_h = nc.dram_tensor("bv", [1, DM], F32, kind="ExternalInput")
    bo_h = nc.dram_tensor("bo", [1, DM], F32, kind="ExternalInput")
    gamma_h = nc.dram_tensor("gamma", [1, DM], F32, kind="ExternalInput")
    beta_h = nc.dram_tensor("beta", [1, DM], F32, kind="ExternalInput")
    gw_h = nc.dram_tensor("gw", [DM, NEXP], F32, kind="ExternalInput")
    gb_h = nc.dram_tensor("gb", [1, NEXP], F32, kind="ExternalInput")
    eb_h = nc.dram_tensor("eb", [NEXP, DM], F32, kind="ExternalInput")
    ew_h = nc.dram_tensor("ew", [NEXP, DM, DM], F32, kind="ExternalInput")
    c_h = nc.dram_tensor("c", [1, DM], F32, kind="ExternalInput")
    wmod_h = nc.dram_tensor("wmod", [DM, 6 * DM], F32, kind="ExternalInput")
    out_h = nc.dram_tensor("out", [T, DM], F32, kind="ExternalOutput")
    wbh = [nc.dram_tensor("wbh%d" % k, [128, NFC * DM], BF16, kind="Internal")
           for k in range(4)]

    es = ExitStack()
    sb = lambda name, shape, dt: es.enter_context(nc.sbuf_tensor(name, shape, dt))

    xsb = sb("xsb", [128, NB * DM], F32)
    h2T = sb("h2T", [128, NFC * T], BF16)
    CAb = sb("CAb", [128, DM], BF16)
    SAb = sb("SAb", [128, DM], BF16)
    CBb = sb("CBb", [128, DM], BF16)
    SBb = sb("SBb", [128, DM], BF16)
    GBb = sb("GBb", [128, DM], BF16)
    GAMGA = sb("GAMGA", [128, DM], BF16)
    BETGA = sb("BETGA", [128, DM], BF16)
    identf = sb("identf", [128, 128], F32)
    ones1f = sb("ones1f", [1, 128], F32)
    ones1b = sb("ones1b", [1, 128], BF16)
    borow_bf = sb("borow_bf", [1, DM], BF16)
    onesb = sb("onesb", [128, 64], BF16)
    onescol = sb("onescol", [128, 1], F32)
    bqcol = sb("bqcol", [128, NFC], F32)
    bkcol = sb("bkcol", [128, NFC], F32)
    bvcol_bf = sb("bvcol_bf", [128, NFC], BF16)
    creshape = sb("creshape", [128, NFC], F32)
    s_col = sb("s_col", [128, NFC], F32)
    gw_bf = sb("gw_bf", [128, NFC * NEXP], BF16)
    GBIASb = sb("GBIASb", [128, NEXP], F32)
    w_gate = sb("w_gate", [128, NB * NEXP], F32)
    wT_bf = sb("wT_bf", [NEXP, T], BF16)
    stats = sb("stats", [128, 96], F32)
    gatef = sb("gatef", [128, NEXP], F32)
    gatee = sb("gatee", [128, NEXP], F32)
    gEM = sb("gEM", [NEXP, 128], F32)
    c0125 = sb("c0125", [128, 1], F32)
    cm3 = sb("cm3", [128, 1], F32)
    cn1 = sb("cn1", [128, 1], F32)
    ceps = sb("ceps", [128, 1], F32)
    cmhalf = sb("cmhalf", [128, 1], F32)
    recipB = sb("recipB", [128, DM], BF16)
    recrow = recipB
    rowB = sb("rowB", [1, DM], F32)

    sems = {}
    for name in ("dx", "dw", "dw2", "dsc", "dsm", "dsg", "dsh", "dew0", "dew1", "dout", "dwb0", "dwb1", "dwh0", "dwh1",
                 "t1", "t2", "v1", "v2", "s1", "s2", "m1", "m2"):
        sems[name] = es.enter_context(nc.semaphore(name))

    pbA = es.enter_context(nc.psum_tensor("pbA", [128, DM], F32))
    pbB = es.enter_context(nc.psum_tensor("pbB", [128, DM], F32))
    pbC = es.enter_context(nc.psum_tensor("pbC", [128, DM], F32))
    pbD = es.enter_context(nc.psum_tensor("pbD", [128, DM], F32))

    P = Prog(nc, sems, None)

    for tt in range(NB):
        P.emit("sync", lambda e, tt=tt: e.dma_start(
            xsb[:, DM * tt:DM * (tt + 1)], x_h.ap()[128 * tt:128 * (tt + 1), :]),
            inc="dx", inc_n=16)

    P.emit("sync", lambda e: e.dma_start(
        creshape[:, :], c_h.ap().rearrange("a (f p) -> p (f a)", p=128),
        allow_slow_non_contiguous=True),
        inc="dsc", inc_n=16)
    P.emit("sync", lambda e: e.dma_start(
        bqcol[:, :], bq_h.ap().rearrange("a (f p) -> p (f a)", p=128),
        allow_slow_non_contiguous=True),
        inc="dsm", inc_n=16)
    P.emit("sync", lambda e: e.dma_start(
        bkcol[:, :], bk_h.ap().rearrange("a (f p) -> p (f a)", p=128),
        allow_slow_non_contiguous=True),
        inc="dsm", inc_n=16)
    P.emit("gpsimd", lambda e: e.dma_start(
        bvcol_bf[:, :], bv_h.ap().rearrange("a (f p) -> p (f a)", p=128),
        allow_slow_non_contiguous=True),
        inc="dsg", inc_n=16)
    DSM_TOTAL = 96
    P.emit("gpsimd", lambda e: e.dma_start(
        gw_bf[:, :].rearrange("p (f g) -> p f g", f=NFC),
        gw_h.ap().rearrange("(f p) g -> p f g", p=128)), inc="dsg", inc_n=16)
    V_DSM = DSM_TOTAL
    V_DSG = 32

    for ms in (lambda e: e.memset(ones1b[:, :], 1.0),
               lambda e: e.memset(ones1f[:, :], 1.0),
               lambda e: e.memset(onesb[:, :], 1.0),
               lambda e: e.memset(onescol[:, :], 1.0),
               lambda e: e.memset(c0125[:, :], 0.125),
               lambda e: e.memset(cm3[:, :], -3.0),
               lambda e: e.memset(cn1[:, :], -1.0),
               lambda e: e.memset(cmhalf[:, :], -0.5),
               lambda e: e.memset(ceps[:, :], EPS)):
        P.emit("gpsimd", ms)
    P.emit("gpsimd", lambda e: e.iota(stats[:, PIDX:PIDX + 1], [[0, 1]],
                                      channel_multiplier=1,
                                      allow_small_or_imprecise_dtypes=True))
    P.emit("gpsimd", lambda e: e.iota(identf[:, :], [[1, 128]], channel_multiplier=0,
                                      allow_small_or_imprecise_dtypes=True), inc="m1")
    vm = P.val("m1")
    P.emit("vector", lambda e: e.tensor_scalar(
        identf[:, :], identf[:, :], stats[:, PIDX:PIDX + 1], None, ALU.is_equal),
        waits=[("m1", vm)], reads=("identf",), writes=("identf",), inc="v1")

    P.emit("scalar", lambda e: e.activation(s_col[:, :], creshape[:, :], AF.Silu),
           waits=[("dsc", 16)], inc="s1")
    S_SILU = P.val("s1")

    with nc.sbuf_tensor("wmstage", [128, 3 * DM], F32) as wmstage, \
         nc.sbuf_tensor("macc", [128, 3 * DM], F32) as macc, \
         nc.sbuf_tensor("gamb", [128, DM], F32) as gamb, \
         nc.sbuf_tensor("gtmp", [128, DM], F32) as gtmp, \
         nc.sbuf_tensor("gs", [128, DM], F32) as gs, \
         nc.sbuf_tensor("mod_row", [1, 6 * DM], F32) as mod_row, \
         nc.sbuf_tensor("grow", [1, NEXP], F32) as grow:
        psmod = pbA[0:1, :]
        psbcb = pbB[:, :]

        P.emit("sync", lambda e: e.dma_start(grow[:, :], gb_h.ap()[:, :]),
               inc="dsm", inc_n=16)

        HW = 1536
        V_RED0 = 0
        kkg = 0
        vstt_hist = {}
        for half in range(2):
            hbase = 3 * DM * half
            for fc in range(NFC):
                for hf in range(2):
                    buf = (kkg % 2) * HW
                    dws = "dw" if kkg % 2 == 0 else "dw2"
                    tw = [("v1", vstt_hist[kkg - 2])] if kkg >= 2 else []
                    P.emit("sync", lambda e, fc=fc, hf=hf, buf=buf, hbase=hbase: e.dma_start(
                        wmstage[:, buf:buf + HW],
                        wmod_h.ap()[128 * fc:128 * (fc + 1),
                                    hbase + HW * hf:hbase + HW * (hf + 1)]),
                        waits=tw, inc=dws, inc_n=16)
                    dwv = (dws, P.val(dws))
                    key = "macc%d" % hf
                    if fc == 0:
                        P.emit("vector", lambda e, hf=hf, buf=buf, fc=fc: e.tensor_scalar(
                            macc[:, HW * hf:HW * (hf + 1)], wmstage[:, buf:buf + HW],
                            s_col[:, fc:fc + 1], None, ALU.mult),
                            waits=[dwv, ("s1", S_SILU), ("t1", V_RED0)],
                            writes=(key,))
                    else:
                        P.emit("vector", lambda e, hf=hf, buf=buf, fc=fc: e.scalar_tensor_tensor(
                            macc[:, HW * hf:HW * (hf + 1)], wmstage[:, buf:buf + HW],
                            s_col[:, fc:fc + 1], macc[:, HW * hf:HW * (hf + 1)],
                            ALU.mult, ALU.add),
                            waits=[dwv], reads=(key,), writes=(key,))
                    P.emit("vector", lambda e: e.engine_nop(), inc="v1")
                    vstt_hist[kkg] = P.val("v1")
                    kkg += 1
            V_MACC = P.val("v1")
            for rnd in range(3):
                for j in range(2):
                    col = rnd * 1024 + j * 512
                    mw = []
                    if rnd == 0 and j == 0:
                        mw.append(("v1", V_MACC))
                    if (half > 0 or rnd > 0) and j == 0:
                        mw.append(("v2", P.val("v2")))
                    P.emit("tensor", lambda e, col=col, j=j: e.matmul(
                        psmod[:, 512 * j:512 * (j + 1)], onescol[:, :],
                        macc[:, col:col + 512], start=True, stop=True),
                        waits=mw, inc="t1")
                tv = P.val("t1")
                P.emit("vector", lambda e, rnd=rnd, hbase=hbase: e.tensor_copy(
                    mod_row[:, hbase + 1024 * rnd:hbase + 1024 * (rnd + 1)],
                    psmod[:, :]),
                    waits=[("t1", tv)], writes=("modrow",), inc="v2")
            P.emit("vector", lambda e: e.engine_nop(), inc="v1")
            if half == 0:
                V_RED0 = P.val("t1")
        V_MODROW = P.val("v1")

        def bcast1024(row_fn, waitv):
            P.emit("tensor", lambda e: e.matmul(
                psbcb[:, 0:512], ones1f[:, :], row_fn(0), start=True, stop=True),
                waits=waitv, inc="t1")
            return P.emit("tensor", lambda e: e.matmul(
                psbcb[:, 512:1024], ones1f[:, :], row_fn(1), start=True, stop=True),
                inc="t1")

        def mslice(base):
            return lambda h: mod_row[:, base + 512 * h: base + 512 * (h + 1)]

        tv = bcast1024(mslice(1 * DM), [("v1", V_MODROW)])
        P.emit("vector", lambda e: e.tensor_scalar(CAb[:, :], psbcb[:, :], 1.0, None, ALU.add),
               waits=[("t1", tv)], inc="v1")
        tv = bcast1024(mslice(0 * DM), [("v1", P.val("v1"))])
        P.emit("vector", lambda e: e.tensor_copy(SAb[:, :], psbcb[:, :]),
               waits=[("t1", tv)], inc="v1")
        tv = bcast1024(mslice(4 * DM), [("v1", P.val("v1"))])
        P.emit("vector", lambda e: e.tensor_scalar(CBb[:, :], psbcb[:, :], 1.0, None, ALU.add),
               waits=[("t1", tv)], inc="v1")
        tv = bcast1024(mslice(3 * DM), [("v1", P.val("v1"))])
        P.emit("vector", lambda e: e.tensor_copy(SBb[:, :], psbcb[:, :]),
               waits=[("t1", tv)], inc="v1")

        def signmax_chain():
            P.emit("vector", lambda e: e.tensor_scalar(gs[:, :], gtmp[:, :], -1.0, None, ALU.mult),
                   reads=("gtmp",), writes=("gs",))
            P.emit("vector", lambda e: e.tensor_max(gs[:, :], gs[:, :], gtmp[:, :]),
                   reads=("gs",), writes=("gs",))
            P.emit("vector", lambda e: e.tensor_scalar(gs[:, :], gs[:, :], 0.25, None, ALU.max),
                   reads=("gs",), writes=("gs",))
            P.emit("vector", lambda e: e.tensor_scalar(gtmp[:, :], gtmp[:, :], 0.0, None, ALU.is_ge),
                   writes=("gtmp",))
            P.emit("vector", lambda e: e.tensor_scalar(gtmp[:, :], gtmp[:, :], 2.0, -1.0, ALU.mult, ALU.add),
                   reads=("gtmp",), writes=("gtmp",))
            P.emit("vector", lambda e: e.tensor_mul(gs[:, :], gs[:, :], gtmp[:, :]),
                   reads=("gtmp",), writes=("gs",))

        tv = bcast1024(mslice(2 * DM), [("v1", P.val("v1"))])
        P.emit("vector", lambda e: e.tensor_copy(gtmp[:, :], psbcb[:, :]),
               waits=[("t1", tv)], writes=("gtmp",))
        signmax_chain()
        P.emit("sync", lambda e: e.dma_start(rowB[:, :], gamma_h.ap()[:, :]),
               inc="dsm", inc_n=16)
        P.emit("vector", lambda e: e.engine_nop(), inc="v1")
        tv = bcast1024(lambda h: rowB[:, 512 * h:512 * (h + 1)],
                       [("v1", P.val("v1")), ("dsm", 64)])
        P.emit("vector", lambda e: e.tensor_copy(gamb[:, :], psbcb[:, :]),
               waits=[("t1", tv)], writes=("gamb",))
        P.emit("vector", lambda e: e.tensor_mul(GAMGA[:, :], gs[:, :], gamb[:, :]),
               reads=("gamb",), inc="v1")
        P.emit("sync", lambda e: e.dma_start(rowB[:, :], beta_h.ap()[:, :]),
               waits=[("t1", tv)], inc="dsm", inc_n=16)
        T1_BETA = tv
        tv = bcast1024(lambda h: rowB[:, 512 * h:512 * (h + 1)],
                       [("v1", P.val("v1")), ("dsm", 80)])
        T1_BETA = tv
        P.emit("vector", lambda e: e.tensor_copy(gamb[:, :], psbcb[:, :]),
               waits=[("t1", tv)], writes=("gamb",))
        P.emit("vector", lambda e: e.tensor_mul(BETGA[:, :], gs[:, :], gamb[:, :]),
               reads=("gamb",), inc="v1")
        tv = bcast1024(mslice(5 * DM), [("v1", P.val("v1"))])
        P.emit("vector", lambda e: e.tensor_copy(gtmp[:, :], psbcb[:, :]),
               waits=[("t1", tv)], writes=("gtmp",))
        signmax_chain()
        P.emit("vector", lambda e: e.tensor_copy(GBb[:, :], gs[:, :]),
               reads=("gs",), inc="v1")
        tv = P.emit("tensor", lambda e: e.matmul(
            psbcb[:, 0:NEXP], ones1f[:, :], grow[:, :], start=True, stop=True),
            waits=[("v1", P.val("v1")), ("dsm", 80)], inc="t1")
        P.emit("vector", lambda e: e.tensor_copy(GBIASb[:, :], psbcb[:, 0:NEXP]),
               waits=[("t1", tv)], inc="v1")

        P.flush_block()

    es2 = ExitStack()
    sb2 = lambda name, shape, dt: es2.enter_context(nc.sbuf_tensor(name, shape, dt))
    wbuf0 = sb2("wbuf0", [128, NFC * DM], BF16)
    wbuf1 = sb2("wbuf1", [128, NFC * DM], BF16)
    wbuf = (wbuf0, wbuf1)
    h_bf = sb2("h_bf", [128, 3 * DM], BF16)
    h0slot = lambda o: (o % 2) * 2 * DM
    xnscr = sb2("xnscr", [128, 2 * DM], BF16)
    sqscr = xnscr
    preLN = sb2("preLN", [128, 2 * DM], BF16)
    hf32 = sb2("hf32", [128, DM], F32)
    h2scr = hf32
    Qasm = sb2("Qasm", [128, 2 * T], BF16)
    Kasm = sb2("Kasm", [128, 2 * T], BF16)
    Vbuf = sb2("Vbuf", [128, 2 * 16 * 65], BF16)
    ctxFM = sb2("ctxFM", [128, 2 * 2048], BF16)
    hT = sb2("hT", [128, 2048], BF16)
    PT = hT

    qview = lambda t: t[:, :].rearrange("p (h g s) -> p h g s", h=2, g=16, s=128)
    qview4 = lambda t: t[:, :].rearrange("p (h g2 pa s) -> p h g2 pa s",
                                         h=2, g2=8, pa=2, s=128)
    vview = Vbuf[:, :].rearrange("p (h g d) -> p h g d", h=2, g=16, d=65)
    cview = ctxFM[:, :].rearrange("p (r c h s) -> p r c h s", r=2, c=8, h=2, s=128)

    def emit_ln(src_fn, key, lane, out_bf_fn, src_key=None, extra_wait=None):
        base = LNW * lane
        c = lambda idx: stats[:, base + idx:base + idx + 1]
        w = ([] if extra_wait is None else [extra_wait])
        rd = (src_key,) if src_key else ()
        P.emit("vector", lambda e: e.bn_stats(stats[:, base:base + 6], src_fn()[:, 0:512]),
               waits=w, reads=rd, writes=(key + "b0",))
        P.emit("vector", lambda e: e.bn_stats(stats[:, base + 6:base + 12], src_fn()[:, 512:1024]),
               writes=(key + "b1",))
        P.emit("vector", lambda e: e.bn_aggr(stats[:, base + 12:base + 14],
                                             stats[:, base:base + 12]),
               reads=(key + "b0", key + "b1"), inc="v2")
        vv = P.val("v2")
        P.emit("scalar", lambda e: e.activation(c(14), c(13), AF.Ln, bias=ceps[:, :]),
               waits=[("v2", vv)], writes=(key + "lnv",))
        P.emit("scalar", lambda e: e.activation(c(15), c(14), AF.Exp, scale=cmhalf[:, :]),
               reads=(key + "lnv",), inc="s2")
        sv = P.val("s2")
        P.emit("vector", lambda e: e.scalar_tensor_tensor(
            c(16), c(12), -1.0, c(15), ALU.mult, ALU.mult),
            waits=[("s2", sv)], writes=(key + "nmr",))
        P.emit("vector", lambda e: e.tensor_scalar(
            out_bf_fn(), src_fn(), c(15), c(16), ALU.mult, ALU.add),
            reads=(key + "nmr",) + rd, inc="v1")
        return ("v1", P.val("v1"))

    ln1_sv = {}
    HD0 = {}

    def emit_h0(o, extra_t1=None):
        hw = [ln1_sv[o][0]]
        if extra_t1 is not None:
            hw.append(("t1", extra_t1))
        P.emit("vector", lambda e: e.tensor_mul(
            hf32[:, 0:DM], Kasm[:, 0:DM], CAb[:, :]),
            waits=hw, writes=("hf",))
        P.emit("vector", lambda e: e.tensor_add(
            hf32[:, 0:DM], hf32[:, 0:DM], SAb[:, :]),
            reads=("hf",), writes=("hf",), inc="v1")
        P.emit("vector", lambda e, o=o: e.tensor_copy(
            h_bf[:, h0slot(o):h0slot(o) + DM], hf32[:, 0:DM]),
            reads=("hf",), inc="v1")
        HD0[o] = P.val("v1")

    def emit_ln1_pair(o):
        svs = []
        for i in range(2):
            tt = 2 * o + i
            sv = emit_ln(lambda tt=tt: xsb[:, DM * tt:DM * (tt + 1)],
                         "ln1_%d" % i, 2 + i,
                         lambda i=i: Kasm[:, DM * i:DM * (i + 1)],
                         extra_wait=(("dx", 16 * NB) if o == 0 else None))
            svs.append(sv)
        ln1_sv[o] = svs

    if True:
        psbv = pbC[0:1, :]
        for k, w_hh in enumerate((wq_h, wk_h, wv_h, wo_h)):
            slot = k % 2
            dwbs = "dwb%d" % slot
            tw = [("dwh%d" % slot, 16 * (k // 2))] if k >= 2 else []
            P.emit("gpsimd", lambda e, w_hh=w_hh, slot=slot: e.dma_start(
                wbuf[slot][:, :].rearrange("p (f o) -> p f o", f=NFC),
                w_hh.ap().rearrange("(f p) o -> p f o", p=128)),
                waits=tw, inc=dwbs, inc_n=16)
            dv = P.val(dwbs)
            P.emit("sync", lambda e, k=k, slot=slot: e.dma_start(
                wbh[k].ap()[:, :], wbuf[slot][:, :]),
                waits=[(dwbs, dv)], inc="dwh%d" % slot, inc_n=16)
        V_WPRE = 32

        n = 0
        for fc in range(NFC):
            for hh in range(2):
                n += 1
                P.emit("tensor", lambda e, fc=fc, hh=hh: e.matmul(
                    psbv[:, 512 * hh:512 * (hh + 1)], bvcol_bf[:, fc:fc + 1],
                    wbuf1[:, DM * fc + 512 * hh: DM * fc + 512 * (hh + 1)],
                    start=(fc == 0), stop=(fc == NFC - 1)),
                    waits=([("dwb1", P.val("dwb1")), ("dsg", V_DSG)] if n == 1 else []),
                    inc=("t1" if n >= 15 else None))
        P.emit("sync", lambda e: e.dma_start(rowB[:, :], bo_h.ap()[:, :]),
               waits=[("t1", T1_BETA)], inc="dsm", inc_n=16)
        tv = P.val("t1")
        P.emit("vector", lambda e: e.tensor_add(rowB[:, 0:DM], rowB[:, 0:DM], psbv[:, :]),
               waits=[("t1", tv), ("dsm", 96)], writes=("rowBB",))
        P.emit("vector", lambda e: e.tensor_copy(borow_bf[:, :], rowB[:, 0:DM]),
               reads=("rowBB",), inc="v1")
        emit_ln1_pair(0)
        emit_h0(0)
        P.flush_block()

    slot_free = {0: None, 1: None}
    wload_v = {}

    def load_weight(oct_, widx):
        slot = widx % 2
        dwbs = "dwb%d" % slot
        tw = [("dwh0", 32), ("dwh1", 32)]
        if slot_free[slot] is not None:
            tw.append(slot_free[slot])
        P.emit("gpsimd", lambda e, widx=widx, slot=slot: e.dma_start(
            wbuf[slot][:, :], wbh[widx].ap()[:, :]),
            waits=tw, inc=dwbs, inc_n=16)
        wload_v[(oct_, widx)] = (dwbs, P.val(dwbs))

    PLV = {}
    H2V = {}
    TAIL_EV = {}
    T1_HT = {}
    TAILT1 = {}
    CHST = {}

    def chain_stage(co, stage):
        st = CHST.setdefault(co, {"a2": {}, "r2": {}, "a3": {}, "r3": {}})
        pv = lambda i: preLN[:, DM * i:DM * (i + 1)]
        xv = lambda i: xnscr[:, DM * i:DM * (i + 1)]
        hv = lambda i: hf32[:, DM * i:DM * (i + 1)]
        c = lambda i, idx: stats[:, LNW * i + idx:LNW * i + idx + 1]
        if stage == 0:
            for i in range(2):
                P.emit("vector", lambda e, i=i: e.bn_stats(
                    stats[:, LNW * i:LNW * i + 6], pv(i)[:, 0:512]),
                    waits=[("v1", PLV[(co, i)])], reads=("pl%d" % i,),
                    writes=("c%db0" % i,))
                P.emit("vector", lambda e, i=i: e.bn_stats(
                    stats[:, LNW * i + 6:LNW * i + 12], pv(i)[:, 512:1024]),
                    reads=("pl%d" % i,), writes=("c%db1" % i,))
                P.emit("vector", lambda e, i=i: e.bn_aggr(
                    stats[:, LNW * i + 12:LNW * i + 14], stats[:, LNW * i:LNW * i + 12]),
                    reads=("c%db0" % i, "c%db1" % i), inc="v2")
                st["a2"][i] = P.val("v2")
        elif stage == 1:
            for i in range(2):
                P.emit("scalar", lambda e, i=i: e.activation(
                    c(i, 14), c(i, 13), AF.Ln, bias=ceps[:, :]),
                    waits=[("v2", st["a2"][i])], writes=("crs%d" % i,))
                P.emit("scalar", lambda e, i=i: e.activation(
                    c(i, 15), c(i, 14), AF.Exp, scale=cmhalf[:, :]),
                    reads=("crs%d" % i,), inc="s2")
                st["r2"][i] = P.val("s2")
        elif stage == 2:
            for i in range(2):
                tt = 2 * co + i
                P.emit("vector", lambda e, i=i: e.scalar_tensor_tensor(
                    c(i, 16), c(i, 12), -1.0, c(i, 15), ALU.mult, ALU.mult),
                    waits=[("s2", st["r2"][i])], writes=("cnm%d" % i,))
                P.emit("vector", lambda e, i=i: e.tensor_scalar(
                    xv(i), pv(i), c(i, 15), c(i, 16), ALU.mult, ALU.add),
                    reads=("cnm%d" % i, "pl%d" % i), writes=("cxn%d" % i,))
                P.emit("vector", lambda e, i=i: e.tensor_mul(
                    pv(i), xv(i), GAMGA[:, :]),
                    reads=("cxn%d" % i,), writes=("pl%d" % i,))
                P.emit("vector", lambda e, i=i: e.tensor_add(
                    pv(i), pv(i), BETGA[:, :]),
                    reads=("pl%d" % i,), writes=("pl%d" % i,))
                P.emit("vector", lambda e, i=i, tt=tt: e.tensor_add(
                    xsb[:, DM * tt:DM * (tt + 1)], xsb[:, DM * tt:DM * (tt + 1)],
                    pv(i)),
                    reads=("pl%d" % i,), writes=("x%d" % tt,), inc="v1")
                P.emit("vector", lambda e, i=i, tt=tt: e.bn_stats(
                    stats[:, LNW * i:LNW * i + 6], xsb[:, DM * tt:DM * tt + 512]),
                    reads=("x%d" % tt,), writes=("c%db0" % i,))
                P.emit("vector", lambda e, i=i, tt=tt: e.bn_stats(
                    stats[:, LNW * i + 6:LNW * i + 12],
                    xsb[:, DM * tt + 512:DM * tt + 1024]),
                    reads=("x%d" % tt,), writes=("c%db1" % i,))
                P.emit("vector", lambda e, i=i: e.bn_aggr(
                    stats[:, LNW * i + 12:LNW * i + 14], stats[:, LNW * i:LNW * i + 12]),
                    reads=("c%db0" % i, "c%db1" % i), inc="v2")
                st["a3"][i] = P.val("v2")
        elif stage == 3:
            for i in range(2):
                P.emit("scalar", lambda e, i=i: e.activation(
                    c(i, 14), c(i, 13), AF.Ln, bias=ceps[:, :]),
                    waits=[("v2", st["a3"][i])], writes=("crs%d" % i,))
                P.emit("scalar", lambda e, i=i: e.activation(
                    c(i, 15), c(i, 14), AF.Exp, scale=cmhalf[:, :]),
                    reads=("crs%d" % i,), inc="s2")
                st["r3"][i] = P.val("s2")
        elif stage == 4:
            t1w = T1_HT[min(co + 1, NOCT - 1)]
            for i in range(2):
                tt = 2 * co + i
                P.emit("vector", lambda e, i=i: e.scalar_tensor_tensor(
                    c(i, 16), c(i, 12), -1.0, c(i, 15), ALU.mult, ALU.mult),
                    waits=[("s2", st["r3"][i])], writes=("cnm%d" % i,))
                P.emit("vector", lambda e, i=i, tt=tt: e.tensor_scalar(
                    xv(i), xsb[:, DM * tt:DM * (tt + 1)], c(i, 15), c(i, 16),
                    ALU.mult, ALU.add),
                    reads=("cnm%d" % i, "x%d" % tt), writes=("cxn%d" % i,))
            emit_h2(co, 0, ("t1", t1w))

    def emit_h2(co, i, t1wait):
        xv = lambda i: xnscr[:, DM * i:DM * (i + 1)]
        P.emit("vector", lambda e, i=i: e.tensor_mul(
            hf32[:, 0:DM], xv(i), CBb[:, :]),
            waits=[t1wait],
            reads=("cxn%d" % i,), writes=("ch2",))
        P.emit("vector", lambda e: e.tensor_add(
            hf32[:, 0:DM], hf32[:, 0:DM], SBb[:, :]),
            reads=("ch2",), writes=("ch2",), inc="v1")
        H2V[(co, i)] = P.val("v1")

    def emit_tail_transposes(co, rc_guard):
        pstcB = (pbD[:, 0:128], pbD[:, 512:640])
        tail_ev = {}
        tail_t1 = {}
        for gp in range(16):
            i_t, cch, tpar = gp // 8, gp % 8, gp % 2
            fw = []
            if gp == 8:
                emit_h2(co, 1, ("t1", tail_t1[7]))
            if gp in (0, 8):
                fw.append(("v1", H2V[(co, i_t)]))
            if gp == 0:
                fw.append(("v1", rc_guard))
            if gp >= 2:
                fw.append(("v1", tail_ev[gp - 2]))
            P.emit("tensor", lambda e, cch=cch, tpar=tpar: e.transpose(
                pstcB[tpar][:, :],
                hf32[:, 128 * cch:128 * (cch + 1)],
                identf[:, :]), waits=fw, inc="t1")
            ttv = P.val("t1")
            tail_t1[gp] = ttv
            tt_prev = 2 * co + i_t
            P.emit("vector", lambda e, cch=cch, tpar=tpar, tt_prev=tt_prev: e.tensor_copy(
                h2T[:, T * cch + 128 * tt_prev:T * cch + 128 * (tt_prev + 1)],
                pstcB[tpar][:, :]),
                waits=[("t1", ttv)], reads=("ch2",), inc="v1")
            tail_ev[gp] = P.val("v1")
        TAIL_EV[co] = tail_ev[15]
        TAILT1[co] = tail_t1[15]

    def emit_gate_tail(co):
        psgt = pbD[:, 256:256 + NEXP]
        psg8 = pbC[0:NEXP, 0:128]
        for i in range(2):
            tt = 2 * co + i
            for fc in range(NFC):
                gwt = []
                if fc == 0:
                    gwt = [("v1", TAIL_EV[co] if i == 0 else P.val("v1")),
                           ("dsg", V_DSG)]
                P.emit("tensor", lambda e, tt=tt, fc=fc: e.matmul(
                    psg8[:, :], gw_bf[:, NEXP * fc:NEXP * (fc + 1)],
                    h2T[:, T * fc + 128 * tt:T * fc + 128 * (tt + 1)],
                    start=(fc == 0), stop=(fc == NFC - 1)),
                    waits=gwt,
                    inc=("m2" if fc == NFC - 1 else None))
            tv = P.val("m2")
            P.emit("vector", lambda e: e.tensor_copy(gEM[:, :], psg8[:, :]),
                   waits=[("m2", tv)], writes=("gEM",), inc="v1")
            vv = P.val("v1")
            P.emit("tensor", lambda e: e.transpose(
                psgt[:, :], gEM[0:NEXP, :], identf[0:NEXP, 0:NEXP]),
                waits=[("v1", vv)], inc="t1")
            tv = P.val("t1")
            c = lambda nm, i=i: stats[:, G0 + GC[nm] + i:G0 + GC[nm] + i + 1]
            P.emit("vector", lambda e: e.tensor_add(gatef[:, :], psgt[:, :], GBIASb[:, :]),
                   waits=[("t1", tv)], writes=("gatef",))
            P.emit("vector", lambda e, i=i: e.reduce_max(c("gmax", i), gatef[:, :], axis=AX.X),
                   reads=("gatef",), writes=("gmax",))
            P.emit("vector", lambda e, i=i: e.tensor_scalar(
                c("gnm", i), c("gmax", i), -1.0, None, ALU.mult),
                reads=("gmax",), inc="v2")
            vv = P.val("v2")
            P.emit("scalar", lambda e, i=i: e.activation(
                gatee[:, :], gatef[:, :], AF.Exp, bias=c("gnm", i)),
                waits=[("v2", vv)], inc="s2")
            sv = P.val("s2")
            P.emit("vector", lambda e, i=i: e.reduce_sum(c("gsum", i), gatee[:, :], axis=AX.X),
                   waits=[("s2", sv)], writes=("gsum",))
            P.emit("vector", lambda e, i=i: e.reciprocal(c("grec", i), c("gsum", i)),
                   reads=("gsum",), writes=("grec",))
            P.emit("vector", lambda e, i=i: e.tensor_scalar(
                gatee[:, :], gatee[:, :], c("grec", i), None, ALU.mult),
                reads=("grec",), writes=("probs",))
            P.emit("vector", lambda e: e.tensor_scalar(
                gatef[:, :], gatee[:, :], 0.1, None, ALU.is_ge),
                reads=("probs",), writes=("sel",))
            P.emit("vector", lambda e: e.tensor_mul(gatee[:, :], gatee[:, :], gatef[:, :]),
                   reads=("sel",), writes=("wsel",))
            P.emit("vector", lambda e, i=i: e.reduce_sum(c("gws", i), gatee[:, :], axis=AX.X),
                   reads=("wsel",), writes=("gws",))
            P.emit("vector", lambda e, i=i: e.tensor_scalar(
                c("gws", i), c("gws", i), 0.05, None, ALU.max),
                reads=("gws",), writes=("gws",))
            P.emit("vector", lambda e, i=i: e.reciprocal(c("gwr", i), c("gws", i)),
                   reads=("gws",), writes=("gwr",))
            P.emit("vector", lambda e, i=i, tt=tt: e.tensor_scalar(
                w_gate[:, NEXP * tt:NEXP * (tt + 1)], gatee[:, :],
                c("gwr", i), None, ALU.mult),
                reads=("gwr",), writes=("wg%d" % tt,), inc="v1")
            vv = P.val("v1")
            P.emit("tensor", lambda e, tt=tt: e.transpose(
                psg8[:, :], w_gate[:, NEXP * tt:NEXP * (tt + 1)], identf[:, :]),
                waits=[("v1", vv)], inc="t1")
            tv = P.val("t1")
            P.emit("vector", lambda e, tt=tt: e.tensor_copy(
                wT_bf[:, 128 * tt:128 * (tt + 1)], psg8[:, :]),
                waits=[("t1", tv)], inc="v1")

    pbd_guard = 0
    for oct_ in range(NOCT):
        if True:
            psq0 = pbA[:, 0:256]
            psq1 = pbA[:, 512:768]
            psv0 = pbB[:, :]
            psv1 = pbC[:, :]
            pst0 = pbD[:, 0:128]
            pst1 = pbD[:, 512:640]
            psq = (psq0, psq1)
            psv = (psv0, psv1)
            pst = (pst0, pst1)

            if oct_ == 0:
                load_weight(0, 0)
                load_weight(0, 1)

            P.emit("vector", lambda e: e.memset(vview[:, :, :, 64:65], 1.0))
            for i in range(2):
                tt = 2 * oct_ + i
                if i == 1:
                    hw = [ln1_sv[oct_][1], ("t1", P.val("t1"))]
                    P.emit("vector", lambda e: e.tensor_mul(
                        hf32[:, 0:DM], Kasm[:, DM:2 * DM], CAb[:, :]),
                        waits=hw, writes=("hf",))
                    P.emit("vector", lambda e: e.tensor_add(
                        hf32[:, 0:DM], hf32[:, 0:DM], SAb[:, :]),
                        reads=("hf",), writes=("hf",), inc="v1")
                    P.emit("vector", lambda e: e.tensor_copy(
                        h_bf[:, DM:2 * DM], hf32[:, 0:DM]),
                        reads=("hf",), inc="v1")
                    hdone_i = P.val("v1")
                else:
                    hdone_i = HD0[oct_]
                ev0 = P.val("v1")
                for cch in range(NFC):
                    k = i * NFC + cch
                    par = k % 2
                    tw = [("v1", hdone_i)]
                    if cch >= 2:
                        tw.append(("v1", ev0 + cch - 1))
                    elif i == 1:
                        tw.append(("v1", ev0 + cch - 1))
                    P.emit("tensor", lambda e, cch=cch, par=par: e.transpose(
                        pst[par][:, :],
                        hf32[:, 128 * cch:128 * (cch + 1)],
                        identf[:, :]), waits=tw, inc="t1")
                    tv = P.val("t1")
                    P.emit("vector", lambda e, i=i, cch=cch, par=par: e.tensor_copy(
                        hT[:, 256 * cch + 128 * i:256 * cch + 128 * (i + 1)],
                        pst[par][:, :]),
                        waits=[("t1", tv)], inc="v1")
            V_HT = P.val("v1")
            T1_HT[oct_] = P.val("t1")

            evac_after = {}
            gidx = 0
            for widx, (bcol, dst) in enumerate(((bqcol, Qasm), (bkcol, Kasm))):
                wtile = wbuf[widx % 2]
                dv = qview(dst)
                for fo in range(NFC):
                    par = gidx % 2
                    for fc in range(NFC):
                        tw = []
                        if fc == 0:
                            if fo == 0:
                                tw += [("v1", V_HT), wload_v[(oct_, widx)]]
                                if widx == 0 and oct_ >= 1:
                                    tw.append(("v1", PLV[(oct_ - 1, 0)]))
                            if gidx >= 2:
                                tw.append(("v1", evac_after[gidx - 2]))
                        P.emit("tensor", lambda e, wtile=wtile, fo=fo, fc=fc, par=par: e.matmul(
                            psq[par][:, :],
                            wtile[:, DM * fc + 128 * fo:DM * fc + 128 * (fo + 1)],
                            hT[:, 256 * fc:256 * (fc + 1)],
                            start=(fc == 0), stop=(fc == NFC - 1)),
                            waits=tw, inc=("m1" if fc == NFC - 1 else None))
                    tv = P.val("m1")
                    evw = [("m1", tv)]
                    if oct_ == 0 and fo == 0:
                        evw.append(("dsm", V_DSM))
                    P.emit("vector", lambda e, dv=dv, fo=fo, par=par, bcol=bcol: e.tensor_scalar(
                        dv[0:64, :, 2 * fo, :],
                        psq[par][0:64, :].rearrange("p (h s) -> p h s", h=2),
                        bcol[0:64, fo:fo + 1], None, ALU.add),
                        waits=evw)
                    P.emit("vector", lambda e, dv=dv, fo=fo, par=par, bcol=bcol: e.tensor_scalar(
                        dv[64:128, :, 2 * fo + 1, :],
                        psq[par][64:128, :].rearrange("p (h s) -> p h s", h=2),
                        bcol[64:128, fo:fo + 1], None, ALU.add),
                        inc="v1")
                    evac_after[gidx] = P.val("v1")
                    gidx += 1
                if widx == 0:
                    slot_free[0] = ("m1", P.val("m1"))
                    load_weight(oct_, 2)
                    dv4 = qview4(Qasm)
                    P.emit("gpsimd", lambda e, dv4=dv4: e.dma_start(
                        dv4[0:64, :, :, 1, :], dv4[64:128, :, :, 1, :]),
                        waits=[("v1", P.val("v1")), ("dsh", P.val("dsh"))],
                        inc="dsh", inc_n=16)
                    V_DSH_Q = P.val("dsh")
                else:
                    slot_free[1] = ("m1", P.val("m1"))
                    load_weight(oct_, 3)
            V_QK = P.val("v1")

            dv4 = qview4(Kasm)
            P.emit("sync", lambda e, dv4=dv4: e.dma_start(
                dv4[0:64, :, :, 1, :], dv4[64:128, :, :, 1, :]),
                waits=[("v1", V_QK)], inc="dsm", inc_n=16)
            V_DSM_K = P.val("dsm")

            for i in range(2):
                for fc in range(NFC):
                    for hh in range(2):
                        tw = []
                        if fc == 0 and hh == 0:
                            tw = [("v1", V_HT), wload_v[(oct_, 2)]]
                            if i == 0 and oct_ >= 1:
                                tw.append(("v1", PLV[(oct_ - 1, 1)]))
                        P.emit("tensor", lambda e, i=i, fc=fc, hh=hh: e.matmul(
                            psv[i][:, 512 * hh:512 * (hh + 1)],
                            hT[:, 256 * fc + 128 * i:256 * fc + 128 * (i + 1)],
                            wbuf0[:, DM * fc + 512 * hh:DM * fc + 512 * (hh + 1)],
                            start=(fc == 0), stop=(fc == NFC - 1)),
                            waits=tw,
                            inc=("m1" if fc == NFC - 1 else None))
                tv = P.val("m1")
                P.emit("vector", lambda e, i=i: e.tensor_copy(
                    vview[:, i, :, 0:64],
                    psv[i][:, :].rearrange("p (g d) -> p g d", g=16)),
                    waits=[("m1", tv)], inc="v1")
                if i == 0:
                    V_PSV0 = P.val("v1")
            V_PSV1 = P.val("v1")
            slot_free[0] = ("m1", P.val("m1"))

            for wu in range(16):
                P.emit("tensor", lambda e: e.matmul(
                    pbD[:, 0:512], CAb[:, 0:128], CAb[:, 0:512],
                    start=True, stop=True),
                    inc=("m1" if wu == 15 else None))

        if True:
            pss0 = pbA[:, :]
            pss1 = pbB[:, :]
            psc = pbC[0:65, :]
            psr = pbD[0:64, :]
            lnrow = recipB
            pss = (pss0, pss1)
            exp_done = {}
            mm2_done = {}
            evac_done = 0
            recip_copied = pbd_guard
            tail_on = oct_ >= 1
            co = oct_ - 1
            tail_ev = {}
            tail_t1 = {}
            pstcB = (pbD[:, 0:128], pbD[:, 512:640])

            def emit_mm2(k, seg):
                h01, gp = k // 32, (k % 16)
                par = k % 2
                for q4 in range(2):
                    tw = [("s1", exp_done[k])] if q4 == 0 else []
                    if gp == 0 and q4 == 0:
                        tw.append(("v1", evac_done if seg > 0 else V_PSV1))
                    P.emit("tensor", lambda e, h01=h01, gp=gp, q4=q4, par=par: e.matmul(
                        psc[:, 512 * q4:512 * (q4 + 1)],
                        vview[:, h01, gp, 0:65],
                        PT[:, 1024 * par + 512 * q4:1024 * par + 512 * (q4 + 1)],
                        start=(gp == 0), stop=(gp == 15)),
                        waits=tw,
                        inc=("t2" if (q4 == 1 or gp == 15) else None))
                mm2_done[k] = P.val("t2")

            pend = None
            for seg in range(4):
                h01, q2 = seg // 2, seg % 2
                if tail_on and seg < 3:
                    chain_stage(co, (0, 2, 4)[seg])
                for gp in range(16):
                    k = seg * 16 + gp
                    par = k % 2
                    for q4 in range(2):
                        tw = []
                        if q4 == 0:
                            if k == 0:
                                tw.append(("dsh", V_DSH_Q))
                                tw.append(("dsm", V_DSM_K))
                            if k == 1:
                                tw.append(("v1", V_PSV0))
                            if k >= 2:
                                tw.append(("s1", exp_done[k - 2]))
                        P.emit("tensor", lambda e, h01=h01, q2=q2, gp=gp, q4=q4, par=par: e.matmul(
                            pss[par][:, 512 * q4:512 * (q4 + 1)],
                            Kasm[0:64, T * h01 + 128 * gp:T * h01 + 128 * (gp + 1)],
                            Qasm[0:64, T * h01 + 1024 * q2 + 512 * q4:
                                 T * h01 + 1024 * q2 + 512 * (q4 + 1)],
                            start=True, stop=True),
                            waits=tw, inc="m1")
                    mm1v = P.val("m1")
                    tw = [("m1", mm1v)]
                    if k >= 2:
                        tw.append(("t2", mm2_done[k - 2]))
                    P.emit("scalar", lambda e, par=par: e.activation(
                        PT[:, 1024 * par:1024 * (par + 1)], pss[par][:, :], AF.Exp,
                        bias=cm3[:, :], scale=c0125[:, :]),
                        waits=tw, inc="s1")
                    exp_done[k] = P.val("s1")
                    if tail_on and gp == 12 and seg <= 1:
                        chain_stage(co, (1, 3)[seg])
                    fw = [("v1", recip_copied)] if gp == 0 else []
                    P.emit("tensor", lambda e: e.matmul(
                        pbD[:, 0:512], CAb[:, 0:128], CAb[:, 0:512],
                        start=True, stop=True), waits=fw,
                        inc=("m1" if gp == 15 else None))
                    if pend is not None:
                        emit_mm2(*pend)
                    pend = (k, seg)
                emit_mm2(*pend)
                pend = None
                for fb in range(8):
                    P.emit("tensor", lambda e: e.matmul(
                        pbD[:, 0:512], CAb[:, 0:128], CAb[:, 0:512],
                        start=True, stop=True),
                        inc=("m1" if fb == 7 else None))

                tv2 = mm2_done[seg * 16 + 15]
                P.emit("scalar", lambda e: e.activation(
                    lnrow[96:97, :], psc[64:65, :], AF.Ln),
                    waits=[("t2", tv2)], writes=("lnrow",))
                P.emit("scalar", lambda e: e.activation(
                    recrow[64:65, :], lnrow[96:97, :], AF.Exp, scale=cn1[64:65, :]),
                    reads=("lnrow",), inc="s2")
                sv = P.val("s2")
                for hh in range(2):
                    tw = []
                    if hh == 0:
                        tw.append(("s2", sv))
                        if seg > 0:
                            tw.append(("v1", recip_copied))
                    P.emit("tensor", lambda e, hh=hh: e.matmul(
                        psr[:, 512 * hh:512 * (hh + 1)], onesb[64:65, :],
                        recrow[64:65, 512 * hh:512 * (hh + 1)], start=True, stop=True),
                        waits=tw, inc="t2")
                tv = P.val("t2")
                P.emit("vector", lambda e: e.tensor_copy(recipB[0:64, :], psr[:, :]),
                       waits=[("t2", tv)], writes=("recipB",), inc="v1")
                recip_copied = P.val("v1")
                pcv = psc[0:64, :].rearrange("p (gb pa s) -> p gb pa s", gb=4, pa=2, s=128)
                rbv = recipB[0:64, :].rearrange("p (gb pa s) -> p gb pa s", gb=4, pa=2, s=128)
                P.emit("vector", lambda e, q2=q2, h01=h01, pcv=pcv, rbv=rbv: e.tensor_tensor(
                    cview[0:64, 0, 4 * q2:4 * (q2 + 1), h01, :],
                    pcv[:, :, 0, :], rbv[:, :, 0, :], ALU.mult),
                    reads=("recipB",))
                P.emit("vector", lambda e, q2=q2, h01=h01, pcv=pcv, rbv=rbv: e.tensor_tensor(
                    cview[0:64, 1, 4 * q2:4 * (q2 + 1), h01, :],
                    pcv[:, :, 1, :], rbv[:, :, 1, :], ALU.mult),
                    inc="v1")
                evac_done = P.val("v1")

            P.emit("gpsimd", lambda e: e.dma_start(
                cview[64:128, 0, :, :, :], cview[0:64, 1, :, :, :]),
                waits=[("v1", evac_done), ("dsh", P.val("dsh"))], inc="dsh", inc_n=16)
            V_DSH_CTX = P.val("dsh")
            V_EXP_LAST = exp_done[63]

        if True:
            if oct_ >= 1:
                emit_tail_transposes(oct_ - 1, recip_copied)
            if oct_ + 1 < NOCT:
                emit_ln1_pair(oct_ + 1)
                emit_h0(oct_ + 1,
                        extra_t1=(TAILT1[oct_ - 1] if oct_ >= 1 else None))
            pso0 = pbA[:, :]
            pso1 = pbB[:, :]
            pso = (pso0, pso1)
            m2v = {}
            for i in range(2):
                for ec in range(NFC):
                    for hh in range(2):
                        tw = []
                        if i == 0 and ec == 0 and hh == 0:
                            tw = [("dsh", V_DSH_CTX), ("s1", V_EXP_LAST),
                                  wload_v[(oct_, 3)]]
                        P.emit("tensor", lambda e, i=i, ec=ec, hh=hh: e.matmul(
                            pso[i][:, 512 * hh:512 * (hh + 1)],
                            cview[:, 0, ec, i, :],
                            wbuf1[:, DM * ec + 512 * hh:DM * ec + 512 * (hh + 1)],
                            start=(ec == 0), stop=False),
                            waits=tw, inc=None)
                for hh in range(2):
                    P.emit("tensor", lambda e, i=i, hh=hh: e.matmul(
                        pso[i][:, 512 * hh:512 * (hh + 1)], ones1b[:, :],
                        borow_bf[:, 512 * hh:512 * (hh + 1)],
                        start=False, stop=True),
                        inc="m2")
                m2v[i] = P.val("m2")
            slot_free[1] = ("m2", P.val("m2"))
            for i in range(2):
                hcol = DM if i == 1 else h0slot(oct_)
                P.emit("vector", lambda e, i=i, hcol=hcol: e.tensor_add(
                    preLN[:, DM * i:DM * (i + 1)], pso[i][:, :],
                    h_bf[:, hcol:hcol + DM]),
                    waits=[("m2", m2v[i])], writes=("pl%d" % i,), inc="v1")
                PLV[(oct_, i)] = P.val("v1")
            if oct_ >= 1:
                emit_gate_tail(oct_ - 1)
            if oct_ + 1 < NOCT:
                load_weight(oct_ + 1, 0)
                load_weight(oct_ + 1, 1)
            pbd_guard = P.val("v1")

    if True:
        for s in range(5):
            chain_stage(NOCT - 1, s)
        emit_tail_transposes(NOCT - 1, 0)
        emit_gate_tail(NOCT - 1)

    es2.close()
    with nc.sbuf_tensor("We0", [128, NFC * DM], BF16) as We0, \
         nc.sbuf_tensor("We1", [128, NFC * DM], BF16) as We1, \
         nc.sbuf_tensor("facc", [128, NB * DM], BF16) as facc, \
         nc.sbuf_tensor("ufin", [128, DM], BF16) as ufin, \
         nc.sbuf_tensor("eb_bf", [NEXP, DM], BF16) as eb_bf:
        pse0 = pbA[:, :]
        pse1 = pbB[:, :]
        P.emit("gpsimd", lambda e: e.dma_start(eb_bf[:, :], eb_h.ap()[:, :]),
               waits=[("m2", P.val("m2"))], inc="dsg", inc_n=16)
        V_EB = P.val("dsg")
        We = (We0, We1)
        pse = (pse0, pse1)
        V_P5START = P.val("v1")
        M2_P5START = P.val("m2")

        binit = {}
        for tt in range(NB):
            par = tt % 2
            for hh in range(2):
                tw = []
                if hh == 0 and tt >= 2:
                    tw.append(("v1", binit[tt - 2]))
                if hh == 0 and tt == 0:
                    tw.append(("dsg", V_EB))
                    tw.append(("v1", V_P5START))
                P.emit("tensor", lambda e, tt=tt, hh=hh, par=par: e.matmul(
                    pse[par][:, 512 * hh:512 * (hh + 1)],
                    wT_bf[:, 128 * tt:128 * (tt + 1)],
                    eb_bf[:, 512 * hh:512 * (hh + 1)], start=True, stop=True),
                    waits=tw, inc="m2")
            tv = P.val("m2")
            P.emit("vector", lambda e, tt=tt, par=par: e.tensor_copy(
                facc[:, DM * tt:DM * (tt + 1)], pse[par][:, :]),
                waits=[("m2", tv)], inc="v1", writes=("fa%d" % tt,))
            binit[tt] = P.val("v1")
        V_BINIT = P.val("v1")

        grp_done = {}
        stt_done = {}
        for e_i in range(NEXP):
            tw = [("m2", grp_done[(e_i - 2, NB - 1)] if e_i >= 2 else M2_P5START)]
            P.emit("gpsimd", lambda e, e_i=e_i: e.dma_start(
                We[e_i % 2][:, :].rearrange("p (f o) -> p f o", f=NFC),
                ew_h.ap()[e_i:e_i + 1, :, :].rearrange(
                    "a (f p) o -> p (a f) o", p=128)),
                waits=tw, inc="dew%d" % (e_i % 2), inc_n=16)
            for tt in range(NB):
                par = tt % 2
                k = e_i * NB + tt
                for fc in range(NFC):
                    for hh in range(2):
                        tw = []
                        if fc == 0 and hh == 0:
                            tw.append(("dew%d" % (e_i % 2), 16 * (e_i // 2 + 1)))
                            if e_i == 0:
                                tw.append(("v1", V_BINIT))
                            if k >= 2:
                                tw.append(("v1", stt_done[k - 2]))
                        P.emit("tensor", lambda e, e_i=e_i, tt=tt, fc=fc, hh=hh, par=par: e.matmul(
                            pse[par][:, 512 * hh:512 * (hh + 1)],
                            h2T[:, T * fc + 128 * tt:T * fc + 128 * (tt + 1)],
                            We[e_i % 2][:, DM * fc + 512 * hh:DM * fc + 512 * (hh + 1)],
                            start=(fc == 0), stop=(fc == NFC - 1)),
                            waits=tw,
                            inc=("m2" if fc == NFC - 1 else None))
                grp_done[(e_i, tt)] = P.val("m2")
                P.emit("vector", lambda e, e_i=e_i, tt=tt, par=par: e.scalar_tensor_tensor(
                    facc[:, DM * tt:DM * (tt + 1)], pse[par][:, :],
                    w_gate[:, NEXP * tt + e_i:NEXP * tt + e_i + 1],
                    facc[:, DM * tt:DM * (tt + 1)], ALU.mult, ALU.add),
                    waits=[("m2", grp_done[(e_i, tt)])], inc="v1",
                    reads=("fa%d" % tt,), writes=("fa%d" % tt,))
                stt_done[k] = P.val("v1")

        for tt in range(NB):
            P.emit("vector", lambda e, tt=tt: e.tensor_mul(
                ufin[:, :], facc[:, DM * tt:DM * (tt + 1)], GBb[:, :]),
                reads=("fa%d" % tt,), writes=("ufin",))
            P.emit("vector", lambda e, tt=tt: e.tensor_add(
                xsb[:, DM * tt:DM * (tt + 1)], xsb[:, DM * tt:DM * (tt + 1)],
                ufin[:, :]),
                reads=("ufin",), inc="v1")
            vv = P.val("v1")
            P.emit("sync", lambda e, tt=tt: e.dma_start(
                out_h.ap()[128 * tt:128 * (tt + 1), :], xsb[:, DM * tt:DM * (tt + 1)]),
                waits=[("v1", vv)], inc="dout", inc_n=16)
        P.emit("sync", lambda e: e.nop(), waits=[("dout", P.val("dout"))])
        P.flush_block()

    es.close()
    return nc


_NC_CACHE = {}


def kernel(L0_attn_x, L0_attn_wq, L0_attn_bq, L0_attn_wk, L0_attn_bk,
           L0_attn_wv, L0_attn_bv, L0_attn_wo, L0_attn_bo, L0_attn_gamma,
           L0_attn_beta, L0_ffn_gate_w, L0_ffn_gate_b, L0_ffn_expert_w,
           L0_ffn_expert_b, L0_c, L0_wmod):
    if "nc" not in _NC_CACHE:
        _NC_CACHE["nc"] = build_program()
    nc = _NC_CACHE["nc"]
    f32 = lambda a: np.ascontiguousarray(np.asarray(a), dtype=np.float32)
    common = {
        "wq": f32(L0_attn_wq), "wk": f32(L0_attn_wk),
        "wv": f32(L0_attn_wv), "wo": f32(L0_attn_wo),
        "bq": f32(L0_attn_bq).reshape(1, DM), "bk": f32(L0_attn_bk).reshape(1, DM),
        "bv": f32(L0_attn_bv).reshape(1, DM), "bo": f32(L0_attn_bo).reshape(1, DM),
        "gamma": f32(L0_attn_gamma).reshape(1, DM),
        "beta": f32(L0_attn_beta).reshape(1, DM),
        "gw": f32(L0_ffn_gate_w), "gb": f32(L0_ffn_gate_b).reshape(1, NEXP),
        "eb": f32(L0_ffn_expert_b), "ew": f32(L0_ffn_expert_w),
        "wmod": f32(L0_wmod),
    }
    xs = f32(L0_attn_x)
    cs = f32(L0_c)
    in_maps = [dict(common, x=xs[b], c=cs[b:b + 1]) for b in range(NCORE)]
    res = run_bass_kernel_spmd(nc, in_maps, core_ids=list(range(NCORE)))
    return np.stack([res.results[b]["out"] for b in range(NCORE)], axis=0)


# version 13
# speedup vs baseline: 1.0795x; 1.0186x over previous
import numpy as np
from contextlib import ExitStack

try:
    import concourse.bass as bass
except ImportError:
    import sys
    sys.path.insert(0, "/opt/trn_rl_repo")
    import concourse.bass as bass
import concourse.mybir as mybir
from concourse.bass_utils import run_bass_kernel_spmd

F32 = mybir.dt.float32
BF16 = mybir.dt.bfloat16
AF = mybir.ActivationFunctionType
ALU = mybir.AluOpType
AX = mybir.AxisListType

NCORE = 8
T = 2048
DM = 1024
NB = 16
NOCT = 8
NFC = 8
NEXP = 8
EPS = 1e-5
INV_DM = 1.0 / DM


class Prog:

    ENGS = ("tensor", "vector", "scalar", "gpsimd", "sync")

    def __init__(self, nc, sems, gap_dummy):
        self.nc = nc
        self.sems = sems
        self.count = {k: 0 for k in sems}
        self.gap = gap_dummy
        self._reset()

    def _reset(self):
        self.ops = {e: [] for e in self.ENGS}
        self.written = {e: set() for e in self.ENGS}
        self.readk = {e: set() for e in self.ENGS}
        self.dirty = {e: False for e in self.ENGS}

    def emit(self, eng, fn, waits=(), inc=None, inc_n=1, reads=(), writes=()):
        if eng in ("vector", "scalar"):
            need = (any(r in self.written[eng] for r in reads)
                    or any(w in self.readk[eng] or w in self.written[eng]
                           for w in writes)
                    or inc is not None)
            if need:
                self.ops[eng].append((lambda e: e.drain(), (), None, 0))
                self.written[eng].clear()
                self.readk[eng].clear()
                self.dirty[eng] = False
        self.ops[eng].append((fn, tuple(waits), inc, inc_n))
        if inc is not None:
            self.count[inc] += inc_n
        elif eng in ("vector", "scalar"):
            self.dirty[eng] = True
        for wkey in writes:
            self.written[eng].add(wkey)
        for rkey in reads:
            self.readk[eng].add(rkey)
        return self.count[inc] if inc else None

    def val(self, sem):
        return self.count[sem]

    def flush_block(self):
        ops, sems = self.ops, self.sems

        def mk(eng_name):
            def run(eng):
                for fn, waits, inc, inc_n in ops[eng_name]:
                    for s, v in waits:
                        eng.wait_ge(sems[s], v)
                    ins = fn(eng)
                    if inc is not None:
                        ins.then_inc(sems[inc], inc_n)
            return run

        with self.nc.Block() as blk:
            blk.tensor(mk("tensor"))
            blk.vector(mk("vector"))
            blk.scalar(mk("scalar"))
            blk.gpsimd(mk("gpsimd"))
            blk.sync(mk("sync"))
        self._reset()


LNW = 20
G0 = 80
GC = {"gmax": 0, "gnm": 2, "gsum": 4, "grec": 6, "gws": 8, "gwr": 10}
PIDX = 94


def build_program():
    nc = bass.Bass("TRN2", target_bir_lowering=False)

    x_h = nc.dram_tensor("x", [T, DM], F32, kind="ExternalInput")
    wq_h = nc.dram_tensor("wq", [DM, DM], F32, kind="ExternalInput")
    wk_h = nc.dram_tensor("wk", [DM, DM], F32, kind="ExternalInput")
    wv_h = nc.dram_tensor("wv", [DM, DM], F32, kind="ExternalInput")
    wo_h = nc.dram_tensor("wo", [DM, DM], F32, kind="ExternalInput")
    bq_h = nc.dram_tensor("bq", [1, DM], F32, kind="ExternalInput")
    bk_h = nc.dram_tensor("bk", [1, DM], F32, kind="ExternalInput")
    bv_h = nc.dram_tensor("bv", [1, DM], F32, kind="ExternalInput")
    bo_h = nc.dram_tensor("bo", [1, DM], F32, kind="ExternalInput")
    gamma_h = nc.dram_tensor("gamma", [1, DM], F32, kind="ExternalInput")
    beta_h = nc.dram_tensor("beta", [1, DM], F32, kind="ExternalInput")
    gw_h = nc.dram_tensor("gw", [DM, NEXP], F32, kind="ExternalInput")
    gb_h = nc.dram_tensor("gb", [1, NEXP], F32, kind="ExternalInput")
    eb_h = nc.dram_tensor("eb", [NEXP, DM], F32, kind="ExternalInput")
    ew_h = nc.dram_tensor("ew", [NEXP, DM, DM], F32, kind="ExternalInput")
    c_h = nc.dram_tensor("c", [1, DM], F32, kind="ExternalInput")
    wmod_h = nc.dram_tensor("wmod", [DM, 6 * DM], F32, kind="ExternalInput")
    out_h = nc.dram_tensor("out", [T, DM], F32, kind="ExternalOutput")
    wbh = [nc.dram_tensor("wbh%d" % k, [128, NFC * DM], BF16, kind="Internal")
           for k in range(4)]

    es = ExitStack()
    sb = lambda name, shape, dt: es.enter_context(nc.sbuf_tensor(name, shape, dt))

    xsb = sb("xsb", [128, NB * DM], F32)
    h2T = sb("h2T", [128, NFC * T], BF16)
    CAb = sb("CAb", [128, DM], BF16)
    SAb = sb("SAb", [128, DM], BF16)
    CBb = sb("CBb", [128, DM], BF16)
    SBb = sb("SBb", [128, DM], BF16)
    GBb = sb("GBb", [128, DM], BF16)
    GAMGA = sb("GAMGA", [128, DM], BF16)
    BETGA = sb("BETGA", [128, DM], BF16)
    identf = sb("identf", [128, 128], F32)
    ones1f = sb("ones1f", [1, 128], F32)
    ones1b = sb("ones1b", [1, 128], BF16)
    borow_bf = sb("borow_bf", [1, DM], BF16)
    onesb = sb("onesb", [128, 64], BF16)
    onescol = sb("onescol", [128, 1], F32)
    bqcol = sb("bqcol", [128, NFC], F32)
    bkcol = sb("bkcol", [128, NFC], F32)
    bvcol_bf = sb("bvcol_bf", [128, NFC], BF16)
    creshape = sb("creshape", [128, NFC], F32)
    s_col = sb("s_col", [128, NFC], F32)
    gw_bf = sb("gw_bf", [128, NFC * NEXP], BF16)
    GBIASb = sb("GBIASb", [128, NEXP], F32)
    w_gate = sb("w_gate", [128, NB * NEXP], F32)
    wT_bf = sb("wT_bf", [NEXP, T], BF16)
    stats = sb("stats", [128, 96], F32)
    gatef = sb("gatef", [128, NEXP], F32)
    gatee = sb("gatee", [128, NEXP], F32)
    gEM = sb("gEM", [NEXP, 128], F32)
    c0125 = sb("c0125", [128, 1], F32)
    cm3 = sb("cm3", [128, 1], F32)
    cn1 = sb("cn1", [128, 1], F32)
    ceps = sb("ceps", [128, 1], F32)
    cmhalf = sb("cmhalf", [128, 1], F32)
    recipB = sb("recipB", [128, DM], BF16)
    recrow = recipB
    rowB = sb("rowB", [1, DM], F32)

    sems = {}
    for name in ("dx", "dw", "dw2", "dsc", "dsm", "dsg", "dsh", "dew0", "dew1", "dout", "dwb0", "dwb1", "dwh0", "dwh1",
                 "t1", "t2", "v1", "v2", "s1", "s2", "m1", "m2"):
        sems[name] = es.enter_context(nc.semaphore(name))

    pbA = es.enter_context(nc.psum_tensor("pbA", [128, DM], F32))
    pbB = es.enter_context(nc.psum_tensor("pbB", [128, DM], F32))
    pbC = es.enter_context(nc.psum_tensor("pbC", [128, DM], F32))
    pbD = es.enter_context(nc.psum_tensor("pbD", [128, DM], F32))

    P = Prog(nc, sems, None)

    for tt in range(NB):
        P.emit("sync", lambda e, tt=tt: e.dma_start(
            xsb[:, DM * tt:DM * (tt + 1)], x_h.ap()[128 * tt:128 * (tt + 1), :]),
            inc="dx", inc_n=16)

    P.emit("sync", lambda e: e.dma_start(
        creshape[:, :], c_h.ap().rearrange("a (f p) -> p (f a)", p=128),
        allow_slow_non_contiguous=True),
        inc="dsc", inc_n=16)
    P.emit("sync", lambda e: e.dma_start(
        bqcol[:, :], bq_h.ap().rearrange("a (f p) -> p (f a)", p=128),
        allow_slow_non_contiguous=True),
        inc="dsm", inc_n=16)
    P.emit("sync", lambda e: e.dma_start(
        bkcol[:, :], bk_h.ap().rearrange("a (f p) -> p (f a)", p=128),
        allow_slow_non_contiguous=True),
        inc="dsm", inc_n=16)
    P.emit("gpsimd", lambda e: e.dma_start(
        bvcol_bf[:, :], bv_h.ap().rearrange("a (f p) -> p (f a)", p=128),
        allow_slow_non_contiguous=True),
        inc="dsg", inc_n=16)
    DSM_TOTAL = 96
    P.emit("gpsimd", lambda e: e.dma_start(
        gw_bf[:, :].rearrange("p (f g) -> p f g", f=NFC),
        gw_h.ap().rearrange("(f p) g -> p f g", p=128)), inc="dsg", inc_n=16)
    V_DSM = DSM_TOTAL
    V_DSG = 32

    for ms in (lambda e: e.memset(ones1b[:, :], 1.0),
               lambda e: e.memset(ones1f[:, :], 1.0),
               lambda e: e.memset(onesb[:, :], 1.0),
               lambda e: e.memset(onescol[:, :], 1.0),
               lambda e: e.memset(c0125[:, :], 0.125),
               lambda e: e.memset(cm3[:, :], -3.0),
               lambda e: e.memset(cn1[:, :], -1.0),
               lambda e: e.memset(cmhalf[:, :], -0.5),
               lambda e: e.memset(ceps[:, :], EPS)):
        P.emit("gpsimd", ms)
    P.emit("gpsimd", lambda e: e.iota(stats[:, PIDX:PIDX + 1], [[0, 1]],
                                      channel_multiplier=1,
                                      allow_small_or_imprecise_dtypes=True))
    P.emit("gpsimd", lambda e: e.iota(identf[:, :], [[1, 128]], channel_multiplier=0,
                                      allow_small_or_imprecise_dtypes=True), inc="m1")
    vm = P.val("m1")
    P.emit("vector", lambda e: e.tensor_scalar(
        identf[:, :], identf[:, :], stats[:, PIDX:PIDX + 1], None, ALU.is_equal),
        waits=[("m1", vm)], reads=("identf",), writes=("identf",), inc="v1")

    P.emit("scalar", lambda e: e.activation(s_col[:, :], creshape[:, :], AF.Silu),
           waits=[("dsc", 16)], inc="s1")
    S_SILU = P.val("s1")

    with nc.sbuf_tensor("wmstage", [128, 3 * DM], F32) as wmstage, \
         nc.sbuf_tensor("macc", [128, 3 * DM], F32) as macc, \
         nc.sbuf_tensor("gamb", [128, DM], F32) as gamb, \
         nc.sbuf_tensor("gtmp", [128, DM], F32) as gtmp, \
         nc.sbuf_tensor("gs", [128, DM], F32) as gs, \
         nc.sbuf_tensor("mod_row", [1, 6 * DM], F32) as mod_row, \
         nc.sbuf_tensor("grow", [1, NEXP], F32) as grow:
        psmod = pbA[0:1, :]
        psbcb = pbB[:, :]

        P.emit("sync", lambda e: e.dma_start(grow[:, :], gb_h.ap()[:, :]),
               inc="dsm", inc_n=16)

        HW = 1536
        V_RED0 = 0
        kkg = 0
        vstt_hist = {}
        for half in range(2):
            hbase = 3 * DM * half
            for fc in range(NFC):
                for hf in range(2):
                    buf = (kkg % 2) * HW
                    dws = "dw" if kkg % 2 == 0 else "dw2"
                    tw = [("v1", vstt_hist[kkg - 2])] if kkg >= 2 else []
                    P.emit("sync", lambda e, fc=fc, hf=hf, buf=buf, hbase=hbase: e.dma_start(
                        wmstage[:, buf:buf + HW],
                        wmod_h.ap()[128 * fc:128 * (fc + 1),
                                    hbase + HW * hf:hbase + HW * (hf + 1)]),
                        waits=tw, inc=dws, inc_n=16)
                    dwv = (dws, P.val(dws))
                    key = "macc%d" % hf
                    if fc == 0:
                        P.emit("vector", lambda e, hf=hf, buf=buf, fc=fc: e.tensor_scalar(
                            macc[:, HW * hf:HW * (hf + 1)], wmstage[:, buf:buf + HW],
                            s_col[:, fc:fc + 1], None, ALU.mult),
                            waits=[dwv, ("s1", S_SILU), ("t1", V_RED0)],
                            writes=(key,))
                    else:
                        P.emit("vector", lambda e, hf=hf, buf=buf, fc=fc: e.scalar_tensor_tensor(
                            macc[:, HW * hf:HW * (hf + 1)], wmstage[:, buf:buf + HW],
                            s_col[:, fc:fc + 1], macc[:, HW * hf:HW * (hf + 1)],
                            ALU.mult, ALU.add),
                            waits=[dwv], reads=(key,), writes=(key,))
                    P.emit("vector", lambda e: e.engine_nop(), inc="v1")
                    vstt_hist[kkg] = P.val("v1")
                    kkg += 1
            V_MACC = P.val("v1")
            for rnd in range(3):
                for j in range(2):
                    col = rnd * 1024 + j * 512
                    mw = []
                    if rnd == 0 and j == 0:
                        mw.append(("v1", V_MACC))
                    if (half > 0 or rnd > 0) and j == 0:
                        mw.append(("v2", P.val("v2")))
                    P.emit("tensor", lambda e, col=col, j=j: e.matmul(
                        psmod[:, 512 * j:512 * (j + 1)], onescol[:, :],
                        macc[:, col:col + 512], start=True, stop=True),
                        waits=mw, inc="t1")
                tv = P.val("t1")
                P.emit("vector", lambda e, rnd=rnd, hbase=hbase: e.tensor_copy(
                    mod_row[:, hbase + 1024 * rnd:hbase + 1024 * (rnd + 1)],
                    psmod[:, :]),
                    waits=[("t1", tv)], writes=("modrow",), inc="v2")
            P.emit("vector", lambda e: e.engine_nop(), inc="v1")
            if half == 0:
                V_RED0 = P.val("t1")
        V_MODROW = P.val("v1")

        def bcast1024(row_fn, waitv):
            P.emit("tensor", lambda e: e.matmul(
                psbcb[:, 0:512], ones1f[:, :], row_fn(0), start=True, stop=True),
                waits=waitv, inc="t1")
            return P.emit("tensor", lambda e: e.matmul(
                psbcb[:, 512:1024], ones1f[:, :], row_fn(1), start=True, stop=True),
                inc="t1")

        def mslice(base):
            return lambda h: mod_row[:, base + 512 * h: base + 512 * (h + 1)]

        tv = bcast1024(mslice(1 * DM), [("v1", V_MODROW)])
        P.emit("vector", lambda e: e.tensor_scalar(CAb[:, :], psbcb[:, :], 1.0, None, ALU.add),
               waits=[("t1", tv)], inc="v1")
        tv = bcast1024(mslice(0 * DM), [("v1", P.val("v1"))])
        P.emit("vector", lambda e: e.tensor_copy(SAb[:, :], psbcb[:, :]),
               waits=[("t1", tv)], inc="v1")
        tv = bcast1024(mslice(4 * DM), [("v1", P.val("v1"))])
        P.emit("vector", lambda e: e.tensor_scalar(CBb[:, :], psbcb[:, :], 1.0, None, ALU.add),
               waits=[("t1", tv)], inc="v1")
        tv = bcast1024(mslice(3 * DM), [("v1", P.val("v1"))])
        P.emit("vector", lambda e: e.tensor_copy(SBb[:, :], psbcb[:, :]),
               waits=[("t1", tv)], inc="v1")

        def signmax_chain():
            P.emit("vector", lambda e: e.tensor_scalar(gs[:, :], gtmp[:, :], -1.0, None, ALU.mult),
                   reads=("gtmp",), writes=("gs",))
            P.emit("vector", lambda e: e.tensor_max(gs[:, :], gs[:, :], gtmp[:, :]),
                   reads=("gs",), writes=("gs",))
            P.emit("vector", lambda e: e.tensor_scalar(gs[:, :], gs[:, :], 0.25, None, ALU.max),
                   reads=("gs",), writes=("gs",))
            P.emit("vector", lambda e: e.tensor_scalar(gtmp[:, :], gtmp[:, :], 0.0, None, ALU.is_ge),
                   writes=("gtmp",))
            P.emit("vector", lambda e: e.tensor_scalar(gtmp[:, :], gtmp[:, :], 2.0, -1.0, ALU.mult, ALU.add),
                   reads=("gtmp",), writes=("gtmp",))
            P.emit("vector", lambda e: e.tensor_mul(gs[:, :], gs[:, :], gtmp[:, :]),
                   reads=("gtmp",), writes=("gs",))

        tv = bcast1024(mslice(2 * DM), [("v1", P.val("v1"))])
        P.emit("vector", lambda e: e.tensor_copy(gtmp[:, :], psbcb[:, :]),
               waits=[("t1", tv)], writes=("gtmp",))
        signmax_chain()
        P.emit("sync", lambda e: e.dma_start(rowB[:, :], gamma_h.ap()[:, :]),
               inc="dsm", inc_n=16)
        P.emit("vector", lambda e: e.engine_nop(), inc="v1")
        tv = bcast1024(lambda h: rowB[:, 512 * h:512 * (h + 1)],
                       [("v1", P.val("v1")), ("dsm", 64)])
        P.emit("vector", lambda e: e.tensor_copy(gamb[:, :], psbcb[:, :]),
               waits=[("t1", tv)], writes=("gamb",))
        P.emit("vector", lambda e: e.tensor_mul(GAMGA[:, :], gs[:, :], gamb[:, :]),
               reads=("gamb",), inc="v1")
        P.emit("sync", lambda e: e.dma_start(rowB[:, :], beta_h.ap()[:, :]),
               waits=[("t1", tv)], inc="dsm", inc_n=16)
        T1_BETA = tv
        tv = bcast1024(lambda h: rowB[:, 512 * h:512 * (h + 1)],
                       [("v1", P.val("v1")), ("dsm", 80)])
        T1_BETA = tv
        P.emit("vector", lambda e: e.tensor_copy(gamb[:, :], psbcb[:, :]),
               waits=[("t1", tv)], writes=("gamb",))
        P.emit("vector", lambda e: e.tensor_mul(BETGA[:, :], gs[:, :], gamb[:, :]),
               reads=("gamb",), inc="v1")
        tv = bcast1024(mslice(5 * DM), [("v1", P.val("v1"))])
        P.emit("vector", lambda e: e.tensor_copy(gtmp[:, :], psbcb[:, :]),
               waits=[("t1", tv)], writes=("gtmp",))
        signmax_chain()
        P.emit("vector", lambda e: e.tensor_copy(GBb[:, :], gs[:, :]),
               reads=("gs",), inc="v1")
        tv = P.emit("tensor", lambda e: e.matmul(
            psbcb[:, 0:NEXP], ones1f[:, :], grow[:, :], start=True, stop=True),
            waits=[("v1", P.val("v1")), ("dsm", 80)], inc="t1")
        P.emit("vector", lambda e: e.tensor_copy(GBIASb[:, :], psbcb[:, 0:NEXP]),
               waits=[("t1", tv)], inc="v1")

        P.flush_block()

    es2 = ExitStack()
    sb2 = lambda name, shape, dt: es2.enter_context(nc.sbuf_tensor(name, shape, dt))
    wbuf0 = sb2("wbuf0", [128, NFC * DM], BF16)
    wbuf1 = sb2("wbuf1", [128, NFC * DM], BF16)
    wbuf = (wbuf0, wbuf1)
    h_bf = sb2("h_bf", [128, 3 * DM], BF16)
    h0slot = lambda o: (o % 2) * 2 * DM
    xnscr = sb2("xnscr", [128, 2 * DM], BF16)
    sqscr = xnscr
    preLN = sb2("preLN", [128, 2 * DM], BF16)
    hf32 = sb2("hf32", [128, DM], F32)
    h2scr = hf32
    Qasm = sb2("Qasm", [128, 2 * T], BF16)
    Kasm = sb2("Kasm", [128, 2 * T], BF16)
    Vbuf = sb2("Vbuf", [128, 2 * 16 * 65], BF16)
    ctxFM = sb2("ctxFM", [128, 2 * 2048], BF16)
    hT = sb2("hT", [128, 2048], BF16)
    PT = hT

    qview = lambda t: t[:, :].rearrange("p (h g s) -> p h g s", h=2, g=16, s=128)
    qview4 = lambda t: t[:, :].rearrange("p (h g2 pa s) -> p h g2 pa s",
                                         h=2, g2=8, pa=2, s=128)
    vview = Vbuf[:, :].rearrange("p (h g d) -> p h g d", h=2, g=16, d=65)
    cview = ctxFM[:, :].rearrange("p (r c h s) -> p r c h s", r=2, c=8, h=2, s=128)

    def emit_ln(src_fn, key, lane, out_bf_fn, src_key=None, extra_wait=None):
        base = LNW * lane
        c = lambda idx: stats[:, base + idx:base + idx + 1]
        w = ([] if extra_wait is None else [extra_wait])
        rd = (src_key,) if src_key else ()
        P.emit("vector", lambda e: e.bn_stats(stats[:, base:base + 6], src_fn()[:, 0:512]),
               waits=w, reads=rd, writes=(key + "b0",))
        P.emit("vector", lambda e: e.bn_stats(stats[:, base + 6:base + 12], src_fn()[:, 512:1024]),
               writes=(key + "b1",))
        P.emit("vector", lambda e: e.bn_aggr(stats[:, base + 12:base + 14],
                                             stats[:, base:base + 12]),
               reads=(key + "b0", key + "b1"), inc="v2")
        vv = P.val("v2")
        P.emit("scalar", lambda e: e.activation(c(14), c(13), AF.Ln, bias=ceps[:, :]),
               waits=[("v2", vv)], writes=(key + "lnv",))
        P.emit("scalar", lambda e: e.activation(c(15), c(14), AF.Exp, scale=cmhalf[:, :]),
               reads=(key + "lnv",), inc="s2")
        sv = P.val("s2")
        P.emit("vector", lambda e: e.scalar_tensor_tensor(
            c(16), c(12), -1.0, c(15), ALU.mult, ALU.mult),
            waits=[("s2", sv)], writes=(key + "nmr",))
        P.emit("vector", lambda e: e.tensor_scalar(
            out_bf_fn(), src_fn(), c(15), c(16), ALU.mult, ALU.add),
            reads=(key + "nmr",) + rd, inc="v1")
        return ("v1", P.val("v1"))

    ln1_sv = {}
    HD0 = {}

    def emit_h0(o, extra_t1=None):
        hw = [ln1_sv[o][0]]
        if extra_t1 is not None:
            hw.append(("t1", extra_t1))
        P.emit("vector", lambda e: e.tensor_mul(
            hf32[:, 0:DM], Kasm[:, 0:DM], CAb[:, :]),
            waits=hw, writes=("hf",))
        P.emit("vector", lambda e: e.tensor_add(
            hf32[:, 0:DM], hf32[:, 0:DM], SAb[:, :]),
            reads=("hf",), writes=("hf",), inc="v1")
        P.emit("vector", lambda e, o=o: e.tensor_copy(
            h_bf[:, h0slot(o):h0slot(o) + DM], hf32[:, 0:DM]),
            reads=("hf",), inc="v1")
        HD0[o] = P.val("v1")

    def emit_ln1_pair(o):
        svs = []
        for i in range(2):
            tt = 2 * o + i
            sv = emit_ln(lambda tt=tt: xsb[:, DM * tt:DM * (tt + 1)],
                         "ln1_%d" % i, 2 + i,
                         lambda i=i: Kasm[:, DM * i:DM * (i + 1)],
                         extra_wait=(("dx", 16 * NB) if o == 0 else None))
            svs.append(sv)
        ln1_sv[o] = svs

    if True:
        psbv = pbC[0:1, :]
        for k, w_hh in enumerate((wq_h, wk_h, wv_h, wo_h)):
            slot = k % 2
            dwbs = "dwb%d" % slot
            tw = [("dwh%d" % slot, 16 * (k // 2))] if k >= 2 else []
            P.emit("gpsimd", lambda e, w_hh=w_hh, slot=slot: e.dma_start(
                wbuf[slot][:, :].rearrange("p (f o) -> p f o", f=NFC),
                w_hh.ap().rearrange("(f p) o -> p f o", p=128)),
                waits=tw, inc=dwbs, inc_n=16)
            dv = P.val(dwbs)
            P.emit("sync", lambda e, k=k, slot=slot: e.dma_start(
                wbh[k].ap()[:, :], wbuf[slot][:, :]),
                waits=[(dwbs, dv)], inc="dwh%d" % slot, inc_n=16)
        V_WPRE = 32

        n = 0
        for fc in range(NFC):
            for hh in range(2):
                n += 1
                P.emit("tensor", lambda e, fc=fc, hh=hh: e.matmul(
                    psbv[:, 512 * hh:512 * (hh + 1)], bvcol_bf[:, fc:fc + 1],
                    wbuf1[:, DM * fc + 512 * hh: DM * fc + 512 * (hh + 1)],
                    start=(fc == 0), stop=(fc == NFC - 1)),
                    waits=([("dwb1", P.val("dwb1")), ("dsg", V_DSG)] if n == 1 else []),
                    inc=("t1" if n >= 15 else None))
        P.emit("sync", lambda e: e.dma_start(rowB[:, :], bo_h.ap()[:, :]),
               waits=[("t1", T1_BETA)], inc="dsm", inc_n=16)
        tv = P.val("t1")
        P.emit("vector", lambda e: e.tensor_add(rowB[:, 0:DM], rowB[:, 0:DM], psbv[:, :]),
               waits=[("t1", tv), ("dsm", 96)], writes=("rowBB",))
        P.emit("vector", lambda e: e.tensor_copy(borow_bf[:, :], rowB[:, 0:DM]),
               reads=("rowBB",), inc="v1")
        emit_ln1_pair(0)
        emit_h0(0)
        P.flush_block()

    slot_free = {0: None, 1: None}
    wload_v = {}

    def load_weight(oct_, widx):
        slot = widx % 2
        dwbs = "dwb%d" % slot
        tw = [("dwh0", 32), ("dwh1", 32)]
        if slot_free[slot] is not None:
            tw.append(slot_free[slot])
        P.emit("gpsimd", lambda e, widx=widx, slot=slot: e.dma_start(
            wbuf[slot][:, :], wbh[widx].ap()[:, :]),
            waits=tw, inc=dwbs, inc_n=16)
        wload_v[(oct_, widx)] = (dwbs, P.val(dwbs))

    PLV = {}
    H2V = {}
    TAIL_EV = {}
    T1_HT = {}
    TAILT1 = {}
    CHST = {}

    def chain_stage(co, stage):
        st = CHST.setdefault(co, {"a2": {}, "r2": {}, "a3": {}, "r3": {}})
        pv = lambda i: preLN[:, DM * i:DM * (i + 1)]
        xv = lambda i: xnscr[:, DM * i:DM * (i + 1)]
        hv = lambda i: hf32[:, DM * i:DM * (i + 1)]
        c = lambda i, idx: stats[:, LNW * i + idx:LNW * i + idx + 1]
        if stage == 0:
            for i in range(2):
                P.emit("vector", lambda e, i=i: e.bn_stats(
                    stats[:, LNW * i:LNW * i + 6], pv(i)[:, 0:512]),
                    waits=[("v1", PLV[(co, i)])], reads=("pl%d" % i,),
                    writes=("c%db0" % i,))
                P.emit("vector", lambda e, i=i: e.bn_stats(
                    stats[:, LNW * i + 6:LNW * i + 12], pv(i)[:, 512:1024]),
                    reads=("pl%d" % i,), writes=("c%db1" % i,))
                P.emit("vector", lambda e, i=i: e.bn_aggr(
                    stats[:, LNW * i + 12:LNW * i + 14], stats[:, LNW * i:LNW * i + 12]),
                    reads=("c%db0" % i, "c%db1" % i), inc="v2")
                st["a2"][i] = P.val("v2")
        elif stage == 1:
            for i in range(2):
                P.emit("scalar", lambda e, i=i: e.activation(
                    c(i, 14), c(i, 13), AF.Ln, bias=ceps[:, :]),
                    waits=[("v2", st["a2"][i])], writes=("crs%d" % i,))
                P.emit("scalar", lambda e, i=i: e.activation(
                    c(i, 15), c(i, 14), AF.Exp, scale=cmhalf[:, :]),
                    reads=("crs%d" % i,), inc="s2")
                st["r2"][i] = P.val("s2")
        elif stage == 2:
            for i in range(2):
                tt = 2 * co + i
                P.emit("vector", lambda e, i=i: e.scalar_tensor_tensor(
                    c(i, 16), c(i, 12), -1.0, c(i, 15), ALU.mult, ALU.mult),
                    waits=[("s2", st["r2"][i])], writes=("cnm%d" % i,))
                P.emit("vector", lambda e, i=i: e.tensor_scalar(
                    xv(i), pv(i), c(i, 15), c(i, 16), ALU.mult, ALU.add),
                    reads=("cnm%d" % i, "pl%d" % i), writes=("cxn%d" % i,))
                P.emit("vector", lambda e, i=i: e.tensor_mul(
                    pv(i), xv(i), GAMGA[:, :]),
                    reads=("cxn%d" % i,), writes=("pl%d" % i,))
                P.emit("vector", lambda e, i=i: e.tensor_add(
                    pv(i), pv(i), BETGA[:, :]),
                    reads=("pl%d" % i,), writes=("pl%d" % i,))
                P.emit("vector", lambda e, i=i, tt=tt: e.tensor_add(
                    xsb[:, DM * tt:DM * (tt + 1)], xsb[:, DM * tt:DM * (tt + 1)],
                    pv(i)),
                    reads=("pl%d" % i,), writes=("x%d" % tt,), inc="v1")
                P.emit("vector", lambda e, i=i, tt=tt: e.bn_stats(
                    stats[:, LNW * i:LNW * i + 6], xsb[:, DM * tt:DM * tt + 512]),
                    reads=("x%d" % tt,), writes=("c%db0" % i,))
                P.emit("vector", lambda e, i=i, tt=tt: e.bn_stats(
                    stats[:, LNW * i + 6:LNW * i + 12],
                    xsb[:, DM * tt + 512:DM * tt + 1024]),
                    reads=("x%d" % tt,), writes=("c%db1" % i,))
                P.emit("vector", lambda e, i=i: e.bn_aggr(
                    stats[:, LNW * i + 12:LNW * i + 14], stats[:, LNW * i:LNW * i + 12]),
                    reads=("c%db0" % i, "c%db1" % i), inc="v2")
                st["a3"][i] = P.val("v2")
        elif stage == 3:
            for i in range(2):
                P.emit("scalar", lambda e, i=i: e.activation(
                    c(i, 14), c(i, 13), AF.Ln, bias=ceps[:, :]),
                    waits=[("v2", st["a3"][i])], writes=("crs%d" % i,))
                P.emit("scalar", lambda e, i=i: e.activation(
                    c(i, 15), c(i, 14), AF.Exp, scale=cmhalf[:, :]),
                    reads=("crs%d" % i,), inc="s2")
                st["r3"][i] = P.val("s2")
        elif stage == 4:
            t1w = T1_HT[min(co + 1, NOCT - 1)]
            for i in range(2):
                tt = 2 * co + i
                P.emit("vector", lambda e, i=i: e.scalar_tensor_tensor(
                    c(i, 16), c(i, 12), -1.0, c(i, 15), ALU.mult, ALU.mult),
                    waits=[("s2", st["r3"][i])], writes=("cnm%d" % i,))
                P.emit("vector", lambda e, i=i, tt=tt: e.tensor_scalar(
                    xv(i), xsb[:, DM * tt:DM * (tt + 1)], c(i, 15), c(i, 16),
                    ALU.mult, ALU.add),
                    reads=("cnm%d" % i, "x%d" % tt), writes=("cxn%d" % i,))
            emit_h2(co, 0, ("t1", t1w))

    def emit_h2(co, i, t1wait):
        xv = lambda i: xnscr[:, DM * i:DM * (i + 1)]
        P.emit("vector", lambda e, i=i: e.tensor_mul(
            hf32[:, 0:DM], xv(i), CBb[:, :]),
            waits=[t1wait],
            reads=("cxn%d" % i,), writes=("ch2",))
        P.emit("vector", lambda e: e.tensor_add(
            hf32[:, 0:DM], hf32[:, 0:DM], SBb[:, :]),
            reads=("ch2",), writes=("ch2",), inc="v1")
        H2V[(co, i)] = P.val("v1")

    def emit_tail_transposes(co, rc_guard):
        pstcB = (pbD[:, 0:128], pbD[:, 512:640])
        tail_ev = {}
        tail_t1 = {}
        for gp in range(16):
            i_t, cch, tpar = gp // 8, gp % 8, gp % 2
            fw = []
            if gp == 8:
                emit_h2(co, 1, ("t1", tail_t1[7]))
            if gp in (0, 8):
                fw.append(("v1", H2V[(co, i_t)]))
            if gp == 0:
                fw.append(("v1", rc_guard))
            if gp >= 2:
                fw.append(("v1", tail_ev[gp - 2]))
            P.emit("tensor", lambda e, cch=cch, tpar=tpar: e.transpose(
                pstcB[tpar][:, :],
                hf32[:, 128 * cch:128 * (cch + 1)],
                identf[:, :]), waits=fw, inc="t1")
            ttv = P.val("t1")
            tail_t1[gp] = ttv
            tt_prev = 2 * co + i_t
            P.emit("vector", lambda e, cch=cch, tpar=tpar, tt_prev=tt_prev: e.tensor_copy(
                h2T[:, T * cch + 128 * tt_prev:T * cch + 128 * (tt_prev + 1)],
                pstcB[tpar][:, :]),
                waits=[("t1", ttv)], reads=("ch2",), inc="v1")
            tail_ev[gp] = P.val("v1")
        TAIL_EV[co] = tail_ev[15]
        TAILT1[co] = tail_t1[15]

    WGV = {}
    WT_EV = {}

    def emit_gate_tail(co):
        psg8 = (pbC[0:NEXP, 0:128], pbC[0:NEXP, 512:640])
        psgt = (pbD[:, 256:256 + NEXP], pbD[:, 288:288 + NEXP])
        gemv = {}
        for i in range(2):
            tt = 2 * co + i
            for fc in range(NFC):
                gwt = []
                if fc == 0:
                    gwt = [("v1", TAIL_EV[co]), ("dsg", V_DSG)]
                P.emit("tensor", lambda e, tt=tt, fc=fc, i=i: e.matmul(
                    psg8[i][:, :], gw_bf[:, NEXP * fc:NEXP * (fc + 1)],
                    h2T[:, T * fc + 128 * tt:T * fc + 128 * (tt + 1)],
                    start=(fc == 0), stop=(fc == NFC - 1)),
                    waits=gwt,
                    inc=("m2" if fc == NFC - 1 else None))
            gemv[i] = P.val("m2")
        for i in range(2):
            P.emit("vector", lambda e, i=i: e.tensor_copy(
                gEM[:, :], psg8[i][:, :]),
                waits=([("m2", gemv[1])] if i == 0 else
                       [("m2", gemv[1]), ("t1", P.val("t1"))]),
                writes=("gEM",), inc="v1")
            vv = P.val("v1")
            P.emit("tensor", lambda e, i=i: e.transpose(
                psgt[i][:, :], gEM[0:NEXP, :], identf[0:NEXP, 0:NEXP]),
                waits=[("v1", vv)], inc="t1")
        TPSGT = P.val("t1")
        for i in range(2):
            tt = 2 * co + i
            c = lambda nm, i=i: stats[:, G0 + GC[nm] + i:G0 + GC[nm] + i + 1]
            P.emit("vector", lambda e, i=i: e.tensor_add(gatef[:, :], psgt[i][:, :], GBIASb[:, :]),
                   waits=([("t1", TPSGT)] if i == 0 else []), writes=("gatef",))
            P.emit("vector", lambda e, i=i: e.reduce_max(c("gmax", i), gatef[:, :], axis=AX.X),
                   reads=("gatef",), writes=("gmax",))
            P.emit("vector", lambda e, i=i: e.tensor_scalar(
                c("gnm", i), c("gmax", i), -1.0, None, ALU.mult),
                reads=("gmax",), inc="v2")
            vv = P.val("v2")
            P.emit("scalar", lambda e, i=i: e.activation(
                gatee[:, :], gatef[:, :], AF.Exp, bias=c("gnm", i)),
                waits=[("v2", vv)], inc="s2")
            sv = P.val("s2")
            P.emit("vector", lambda e, i=i: e.reduce_sum(c("gsum", i), gatee[:, :], axis=AX.X),
                   waits=[("s2", sv)], writes=("gsum",))
            P.emit("vector", lambda e, i=i: e.reciprocal(c("grec", i), c("gsum", i)),
                   reads=("gsum",), writes=("grec",))
            P.emit("vector", lambda e, i=i: e.tensor_scalar(
                gatee[:, :], gatee[:, :], c("grec", i), None, ALU.mult),
                reads=("grec",), writes=("probs",))
            P.emit("vector", lambda e: e.tensor_scalar(
                gatef[:, :], gatee[:, :], 0.1, None, ALU.is_ge),
                reads=("probs",), writes=("sel",))
            P.emit("vector", lambda e: e.tensor_mul(gatee[:, :], gatee[:, :], gatef[:, :]),
                   reads=("sel",), writes=("wsel",))
            P.emit("vector", lambda e, i=i: e.reduce_sum(c("gws", i), gatee[:, :], axis=AX.X),
                   reads=("wsel",), writes=("gws",))
            P.emit("vector", lambda e, i=i: e.tensor_scalar(
                c("gws", i), c("gws", i), 0.05, None, ALU.max),
                reads=("gws",), writes=("gws",))
            P.emit("vector", lambda e, i=i: e.reciprocal(c("gwr", i), c("gws", i)),
                   reads=("gws",), writes=("gwr",))
            P.emit("vector", lambda e, i=i, tt=tt: e.tensor_scalar(
                w_gate[:, NEXP * tt:NEXP * (tt + 1)], gatee[:, :],
                c("gwr", i), None, ALU.mult),
                reads=("gwr",), writes=("wg%d" % tt,), inc="v1")
            WGV[(co, i)] = P.val("v1")

    def emit_gate_wT(co):
        psg8 = (pbC[0:NEXP, 0:128], pbC[0:NEXP, 512:640])
        for i in range(2):
            tt = 2 * co + i
            P.emit("tensor", lambda e, tt=tt, i=i: e.transpose(
                psg8[i][:, :], w_gate[:, NEXP * tt:NEXP * (tt + 1)], identf[:, :]),
                waits=[("v1", WGV[(co, i)])], inc="t1")
            tv = P.val("t1")
            P.emit("vector", lambda e, tt=tt, i=i: e.tensor_copy(
                wT_bf[:, 128 * tt:128 * (tt + 1)], psg8[i][:, :]),
                waits=[("t1", tv)], inc="v1")
        WT_EV[co] = P.val("v1")

    pbd_guard = 0
    for oct_ in range(NOCT):
        if True:
            psq0 = pbA[:, 0:256]
            psq1 = pbA[:, 512:768]
            psv0 = pbB[:, :]
            psv1 = pbC[:, :]
            pst0 = pbD[:, 0:128]
            pst1 = pbD[:, 512:640]
            psq = (psq0, psq1)
            psv = (psv0, psv1)
            pst = (pst0, pst1)

            if oct_ == 0:
                load_weight(0, 0)
                load_weight(0, 1)

            P.emit("vector", lambda e: e.memset(vview[:, :, :, 64:65], 1.0))
            for i in range(2):
                tt = 2 * oct_ + i
                if i == 1:
                    hw = [ln1_sv[oct_][1], ("t1", P.val("t1"))]
                    P.emit("vector", lambda e: e.tensor_mul(
                        hf32[:, 0:DM], Kasm[:, DM:2 * DM], CAb[:, :]),
                        waits=hw, writes=("hf",))
                    P.emit("vector", lambda e: e.tensor_add(
                        hf32[:, 0:DM], hf32[:, 0:DM], SAb[:, :]),
                        reads=("hf",), writes=("hf",), inc="v1")
                    P.emit("vector", lambda e: e.tensor_copy(
                        h_bf[:, DM:2 * DM], hf32[:, 0:DM]),
                        reads=("hf",), inc="v1")
                    hdone_i = P.val("v1")
                else:
                    hdone_i = HD0[oct_]
                ev0 = P.val("v1")
                for cch in range(NFC):
                    k = i * NFC + cch
                    par = k % 2
                    tw = [("v1", hdone_i)]
                    if cch >= 2:
                        tw.append(("v1", ev0 + cch - 1))
                    elif i == 1:
                        tw.append(("v1", ev0 + cch - 1))
                    P.emit("tensor", lambda e, cch=cch, par=par: e.transpose(
                        pst[par][:, :],
                        hf32[:, 128 * cch:128 * (cch + 1)],
                        identf[:, :]), waits=tw, inc="t1")
                    tv = P.val("t1")
                    P.emit("vector", lambda e, i=i, cch=cch, par=par: e.tensor_copy(
                        hT[:, 256 * cch + 128 * i:256 * cch + 128 * (i + 1)],
                        pst[par][:, :]),
                        waits=[("t1", tv)], inc="v1")
            V_HT = P.val("v1")
            T1_HT[oct_] = P.val("t1")

            evac_after = {}
            gidx = 0
            for widx, (bcol, dst) in enumerate(((bqcol, Qasm), (bkcol, Kasm))):
                wtile = wbuf[widx % 2]
                dv = qview(dst)
                for fo in range(NFC):
                    par = gidx % 2
                    for fc in range(NFC):
                        tw = []
                        if fc == 0:
                            if fo == 0:
                                tw += [("v1", V_HT), wload_v[(oct_, widx)]]
                                if widx == 0 and oct_ >= 1:
                                    tw.append(("v1", PLV[(oct_ - 1, 0)]))
                            if gidx >= 2:
                                tw.append(("v1", evac_after[gidx - 2]))
                        P.emit("tensor", lambda e, wtile=wtile, fo=fo, fc=fc, par=par: e.matmul(
                            psq[par][:, :],
                            wtile[:, DM * fc + 128 * fo:DM * fc + 128 * (fo + 1)],
                            hT[:, 256 * fc:256 * (fc + 1)],
                            start=(fc == 0), stop=(fc == NFC - 1)),
                            waits=tw, inc=("m1" if fc == NFC - 1 else None))
                    tv = P.val("m1")
                    evw = [("m1", tv)]
                    if oct_ == 0 and fo == 0:
                        evw.append(("dsm", V_DSM))
                    P.emit("vector", lambda e, dv=dv, fo=fo, par=par, bcol=bcol: e.tensor_scalar(
                        dv[0:64, :, 2 * fo, :],
                        psq[par][0:64, :].rearrange("p (h s) -> p h s", h=2),
                        bcol[0:64, fo:fo + 1], None, ALU.add),
                        waits=evw)
                    P.emit("vector", lambda e, dv=dv, fo=fo, par=par, bcol=bcol: e.tensor_scalar(
                        dv[64:128, :, 2 * fo + 1, :],
                        psq[par][64:128, :].rearrange("p (h s) -> p h s", h=2),
                        bcol[64:128, fo:fo + 1], None, ALU.add),
                        inc="v1")
                    evac_after[gidx] = P.val("v1")
                    gidx += 1
                if widx == 0:
                    slot_free[0] = ("m1", P.val("m1"))
                    load_weight(oct_, 2)
                    dv4 = qview4(Qasm)
                    P.emit("gpsimd", lambda e, dv4=dv4: e.dma_start(
                        dv4[0:64, :, :, 1, :], dv4[64:128, :, :, 1, :]),
                        waits=[("v1", P.val("v1")), ("dsh", P.val("dsh"))],
                        inc="dsh", inc_n=16)
                    V_DSH_Q = P.val("dsh")
                else:
                    slot_free[1] = ("m1", P.val("m1"))
                    load_weight(oct_, 3)
            V_QK = P.val("v1")

            dv4 = qview4(Kasm)
            P.emit("sync", lambda e, dv4=dv4: e.dma_start(
                dv4[0:64, :, :, 1, :], dv4[64:128, :, :, 1, :]),
                waits=[("v1", V_QK)], inc="dsm", inc_n=16)
            V_DSM_K = P.val("dsm")

            for i in range(2):
                for fc in range(NFC):
                    for hh in range(2):
                        tw = []
                        if fc == 0 and hh == 0:
                            tw = [("v1", V_HT), wload_v[(oct_, 2)]]
                            if i == 0 and oct_ >= 1:
                                tw.append(("v1", PLV[(oct_ - 1, 1)]))

                        P.emit("tensor", lambda e, i=i, fc=fc, hh=hh: e.matmul(
                            psv[i][:, 512 * hh:512 * (hh + 1)],
                            hT[:, 256 * fc + 128 * i:256 * fc + 128 * (i + 1)],
                            wbuf0[:, DM * fc + 512 * hh:DM * fc + 512 * (hh + 1)],
                            start=(fc == 0), stop=(fc == NFC - 1)),
                            waits=tw,
                            inc=("m1" if fc == NFC - 1 else None))
                tv = P.val("m1")
                P.emit("vector", lambda e, i=i: e.tensor_copy(
                    vview[:, i, :, 0:64],
                    psv[i][:, :].rearrange("p (g d) -> p g d", g=16)),
                    waits=[("m1", tv)], inc="v1")
                if i == 0:
                    V_PSV0 = P.val("v1")
            V_PSV1 = P.val("v1")
            slot_free[0] = ("m1", P.val("m1"))

            for wu in range(16):
                P.emit("tensor", lambda e: e.matmul(
                    pbD[:, 0:512], CAb[:, 0:128], CAb[:, 0:512],
                    start=True, stop=True),
                    inc=("m1" if wu == 15 else None))

        if True:
            pss0 = pbA[:, :]
            pss1 = pbB[:, :]
            psc = pbC[0:65, :]
            psr = pbD[0:64, :]
            lnrow = recipB
            pss = (pss0, pss1)
            exp_done = {}
            mm2_done = {}
            evac_done = 0
            recip_copied = pbd_guard
            tail_on = oct_ >= 1
            co = oct_ - 1
            tail_ev = {}
            tail_t1 = {}
            pstcB = (pbD[:, 0:128], pbD[:, 512:640])

            def emit_mm2(k, seg):
                h01, gp = k // 32, (k % 16)
                par = k % 2
                for q4 in range(2):
                    tw = [("s1", exp_done[k])] if q4 == 0 else []
                    if gp == 0 and q4 == 0:
                        tw.append(("v1", evac_done if seg > 0 else V_PSV1))
                    P.emit("tensor", lambda e, h01=h01, gp=gp, q4=q4, par=par: e.matmul(
                        psc[:, 512 * q4:512 * (q4 + 1)],
                        vview[:, h01, gp, 0:65],
                        PT[:, 1024 * par + 512 * q4:1024 * par + 512 * (q4 + 1)],
                        start=(gp == 0), stop=(gp == 15)),
                        waits=tw,
                        inc=("t2" if (q4 == 1 or gp == 15) else None))
                mm2_done[k] = P.val("t2")

            pend = None
            for seg in range(4):
                h01, q2 = seg // 2, seg % 2
                if tail_on and seg < 3:
                    chain_stage(co, (0, 2, 4)[seg])
                for gp in range(16):
                    k = seg * 16 + gp
                    par = k % 2
                    for q4 in range(2):
                        tw = []
                        if q4 == 0:
                            if k == 0:
                                tw.append(("dsh", V_DSH_Q))
                                tw.append(("dsm", V_DSM_K))
                            if k == 1:
                                tw.append(("v1", V_PSV0))
                            if k >= 2:
                                tw.append(("s1", exp_done[k - 2]))
                        P.emit("tensor", lambda e, h01=h01, q2=q2, gp=gp, q4=q4, par=par: e.matmul(
                            pss[par][:, 512 * q4:512 * (q4 + 1)],
                            Kasm[0:64, T * h01 + 128 * gp:T * h01 + 128 * (gp + 1)],
                            Qasm[0:64, T * h01 + 1024 * q2 + 512 * q4:
                                 T * h01 + 1024 * q2 + 512 * (q4 + 1)],
                            start=True, stop=True),
                            waits=tw, inc="m1")
                    mm1v = P.val("m1")
                    tw = [("m1", mm1v)]
                    if k >= 2:
                        tw.append(("t2", mm2_done[k - 2]))
                    P.emit("scalar", lambda e, par=par: e.activation(
                        PT[:, 1024 * par:1024 * (par + 1)], pss[par][:, :], AF.Exp,
                        bias=cm3[:, :], scale=c0125[:, :]),
                        waits=tw, inc="s1")
                    exp_done[k] = P.val("s1")
                    if tail_on and gp == 12 and seg <= 1:
                        chain_stage(co, (1, 3)[seg])
                    fw = [("v1", recip_copied)] if gp == 0 else []
                    P.emit("tensor", lambda e: e.matmul(
                        pbD[:, 0:512], CAb[:, 0:128], CAb[:, 0:512],
                        start=True, stop=True), waits=fw,
                        inc=("m1" if gp == 15 else None))
                    if pend is not None:
                        emit_mm2(*pend)
                    pend = (k, seg)
                emit_mm2(*pend)
                pend = None
                for fb in range(8):
                    P.emit("tensor", lambda e: e.matmul(
                        pbD[:, 0:512], CAb[:, 0:128], CAb[:, 0:512],
                        start=True, stop=True),
                        inc=("m1" if fb == 7 else None))

                tv2 = mm2_done[seg * 16 + 15]
                P.emit("scalar", lambda e: e.activation(
                    lnrow[96:97, :], psc[64:65, :], AF.Ln),
                    waits=[("t2", tv2)], writes=("lnrow",))
                P.emit("scalar", lambda e: e.activation(
                    recrow[64:65, :], lnrow[96:97, :], AF.Exp, scale=cn1[64:65, :]),
                    reads=("lnrow",), inc="s2")
                sv = P.val("s2")
                for hh in range(2):
                    tw = []
                    if hh == 0:
                        tw.append(("s2", sv))
                        if seg > 0:
                            tw.append(("v1", recip_copied))
                    P.emit("tensor", lambda e, hh=hh: e.matmul(
                        psr[:, 512 * hh:512 * (hh + 1)], onesb[64:65, :],
                        recrow[64:65, 512 * hh:512 * (hh + 1)], start=True, stop=True),
                        waits=tw, inc="t2")
                tv = P.val("t2")
                P.emit("vector", lambda e: e.tensor_copy(recipB[0:64, :], psr[:, :]),
                       waits=[("t2", tv)], writes=("recipB",), inc="v1")
                recip_copied = P.val("v1")
                pcv = psc[0:64, :].rearrange("p (gb pa s) -> p gb pa s", gb=4, pa=2, s=128)
                rbv = recipB[0:64, :].rearrange("p (gb pa s) -> p gb pa s", gb=4, pa=2, s=128)
                P.emit("vector", lambda e, q2=q2, h01=h01, pcv=pcv, rbv=rbv: e.tensor_tensor(
                    cview[0:64, 0, 4 * q2:4 * (q2 + 1), h01, :],
                    pcv[:, :, 0, :], rbv[:, :, 0, :], ALU.mult),
                    reads=("recipB",))
                P.emit("vector", lambda e, q2=q2, h01=h01, pcv=pcv, rbv=rbv: e.tensor_tensor(
                    cview[0:64, 1, 4 * q2:4 * (q2 + 1), h01, :],
                    pcv[:, :, 1, :], rbv[:, :, 1, :], ALU.mult),
                    inc="v1")
                evac_done = P.val("v1")

            P.emit("gpsimd", lambda e: e.dma_start(
                cview[64:128, 0, :, :, :], cview[0:64, 1, :, :, :]),
                waits=[("v1", evac_done), ("dsh", P.val("dsh"))], inc="dsh", inc_n=16)
            V_DSH_CTX = P.val("dsh")
            V_EXP_LAST = exp_done[63]

        if True:
            if oct_ >= 1:
                emit_tail_transposes(oct_ - 1, recip_copied)
            if oct_ + 1 < NOCT:
                emit_ln1_pair(oct_ + 1)
                emit_h0(oct_ + 1,
                        extra_t1=(TAILT1[oct_ - 1] if oct_ >= 1 else None))
            pso0 = pbA[:, :]
            pso1 = pbB[:, :]
            pso = (pso0, pso1)
            m2v = {}
            for i in range(2):
                for ec in range(NFC):
                    for hh in range(2):
                        tw = []
                        if i == 0 and ec == 0 and hh == 0:
                            tw = [("dsh", V_DSH_CTX), ("s1", V_EXP_LAST),
                                  wload_v[(oct_, 3)]]
                        P.emit("tensor", lambda e, i=i, ec=ec, hh=hh: e.matmul(
                            pso[i][:, 512 * hh:512 * (hh + 1)],
                            cview[:, 0, ec, i, :],
                            wbuf1[:, DM * ec + 512 * hh:DM * ec + 512 * (hh + 1)],
                            start=(ec == 0), stop=False),
                            waits=tw, inc=None)
                for hh in range(2):
                    P.emit("tensor", lambda e, i=i, hh=hh: e.matmul(
                        pso[i][:, 512 * hh:512 * (hh + 1)], ones1b[:, :],
                        borow_bf[:, 512 * hh:512 * (hh + 1)],
                        start=False, stop=True),
                        inc="m2")
                m2v[i] = P.val("m2")
            slot_free[1] = ("m2", P.val("m2"))
            for i in range(2):
                hcol = DM if i == 1 else h0slot(oct_)
                P.emit("vector", lambda e, i=i, hcol=hcol: e.tensor_add(
                    preLN[:, DM * i:DM * (i + 1)], pso[i][:, :],
                    h_bf[:, hcol:hcol + DM]),
                    waits=[("m2", m2v[i])], writes=("pl%d" % i,), inc="v1")
                PLV[(oct_, i)] = P.val("v1")
            if oct_ >= 1:
                emit_gate_tail(oct_ - 1)
                emit_gate_wT(oct_ - 1)
            if oct_ + 1 < NOCT:
                load_weight(oct_ + 1, 0)
                load_weight(oct_ + 1, 1)
            pbd_guard = P.val("v1")

    if True:
        for s in range(5):
            chain_stage(NOCT - 1, s)
        emit_tail_transposes(NOCT - 1, 0)
        emit_gate_tail(NOCT - 1)
        emit_gate_wT(NOCT - 1)

    es2.close()
    with nc.sbuf_tensor("We0", [128, NFC * DM], BF16) as We0, \
         nc.sbuf_tensor("We1", [128, NFC * DM], BF16) as We1, \
         nc.sbuf_tensor("facc", [128, NB * DM], BF16) as facc, \
         nc.sbuf_tensor("ufin", [128, DM], BF16) as ufin, \
         nc.sbuf_tensor("eb_bf", [NEXP, DM], BF16) as eb_bf:
        pse0 = pbA[:, :]
        pse1 = pbB[:, :]
        P.emit("gpsimd", lambda e: e.dma_start(eb_bf[:, :], eb_h.ap()[:, :]),
               waits=[("m2", P.val("m2"))], inc="dsg", inc_n=16)
        V_EB = P.val("dsg")
        We = (We0, We1)
        pse = (pse0, pse1)
        V_P5START = P.val("v1")
        M2_P5START = P.val("m2")

        binit = {}
        for tt in range(NB):
            par = tt % 2
            for hh in range(2):
                tw = []
                if hh == 0 and tt >= 2:
                    tw.append(("v1", binit[tt - 2]))
                if hh == 0 and tt == 0:
                    tw.append(("dsg", V_EB))
                    tw.append(("v1", V_P5START))
                P.emit("tensor", lambda e, tt=tt, hh=hh, par=par: e.matmul(
                    pse[par][:, 512 * hh:512 * (hh + 1)],
                    wT_bf[:, 128 * tt:128 * (tt + 1)],
                    eb_bf[:, 512 * hh:512 * (hh + 1)], start=True, stop=True),
                    waits=tw, inc="m2")
            tv = P.val("m2")
            P.emit("vector", lambda e, tt=tt, par=par: e.tensor_copy(
                facc[:, DM * tt:DM * (tt + 1)], pse[par][:, :]),
                waits=[("m2", tv)], inc="v1", writes=("fa%d" % tt,))
            binit[tt] = P.val("v1")
        V_BINIT = P.val("v1")

        grp_done = {}
        stt_done = {}
        for e_i in range(NEXP):
            tw = [("m2", grp_done[(e_i - 2, NB - 1)] if e_i >= 2 else M2_P5START)]
            P.emit("gpsimd", lambda e, e_i=e_i: e.dma_start(
                We[e_i % 2][:, :].rearrange("p (f o) -> p f o", f=NFC),
                ew_h.ap()[e_i:e_i + 1, :, :].rearrange(
                    "a (f p) o -> p (a f) o", p=128)),
                waits=tw, inc="dew%d" % (e_i % 2), inc_n=16)
            for tt in range(NB):
                par = tt % 2
                k = e_i * NB + tt
                for fc in range(NFC):
                    for hh in range(2):
                        tw = []
                        if fc == 0 and hh == 0:
                            tw.append(("dew%d" % (e_i % 2), 16 * (e_i // 2 + 1)))
                            if e_i == 0:
                                tw.append(("v1", V_BINIT))
                            if k >= 2:
                                tw.append(("v1", stt_done[k - 2]))
                        P.emit("tensor", lambda e, e_i=e_i, tt=tt, fc=fc, hh=hh, par=par: e.matmul(
                            pse[par][:, 512 * hh:512 * (hh + 1)],
                            h2T[:, T * fc + 128 * tt:T * fc + 128 * (tt + 1)],
                            We[e_i % 2][:, DM * fc + 512 * hh:DM * fc + 512 * (hh + 1)],
                            start=(fc == 0), stop=(fc == NFC - 1)),
                            waits=tw,
                            inc=("m2" if fc == NFC - 1 else None))
                grp_done[(e_i, tt)] = P.val("m2")
                P.emit("vector", lambda e, e_i=e_i, tt=tt, par=par: e.scalar_tensor_tensor(
                    facc[:, DM * tt:DM * (tt + 1)], pse[par][:, :],
                    w_gate[:, NEXP * tt + e_i:NEXP * tt + e_i + 1],
                    facc[:, DM * tt:DM * (tt + 1)], ALU.mult, ALU.add),
                    waits=[("m2", grp_done[(e_i, tt)])], inc="v1",
                    reads=("fa%d" % tt,), writes=("fa%d" % tt,))
                stt_done[k] = P.val("v1")

        for tt in range(NB):
            P.emit("vector", lambda e, tt=tt: e.tensor_mul(
                ufin[:, :], facc[:, DM * tt:DM * (tt + 1)], GBb[:, :]),
                reads=("fa%d" % tt,), writes=("ufin",))
            P.emit("vector", lambda e, tt=tt: e.tensor_add(
                xsb[:, DM * tt:DM * (tt + 1)], xsb[:, DM * tt:DM * (tt + 1)],
                ufin[:, :]),
                reads=("ufin",), inc="v1")
            vv = P.val("v1")
            P.emit("sync", lambda e, tt=tt: e.dma_start(
                out_h.ap()[128 * tt:128 * (tt + 1), :], xsb[:, DM * tt:DM * (tt + 1)]),
                waits=[("v1", vv)], inc="dout", inc_n=16)
        P.emit("sync", lambda e: e.nop(), waits=[("dout", P.val("dout"))])
        P.flush_block()

    es.close()
    return nc


_NC_CACHE = {}


def kernel(L0_attn_x, L0_attn_wq, L0_attn_bq, L0_attn_wk, L0_attn_bk,
           L0_attn_wv, L0_attn_bv, L0_attn_wo, L0_attn_bo, L0_attn_gamma,
           L0_attn_beta, L0_ffn_gate_w, L0_ffn_gate_b, L0_ffn_expert_w,
           L0_ffn_expert_b, L0_c, L0_wmod):
    if "nc" not in _NC_CACHE:
        _NC_CACHE["nc"] = build_program()
    nc = _NC_CACHE["nc"]
    f32 = lambda a: np.ascontiguousarray(np.asarray(a), dtype=np.float32)
    common = {
        "wq": f32(L0_attn_wq), "wk": f32(L0_attn_wk),
        "wv": f32(L0_attn_wv), "wo": f32(L0_attn_wo),
        "bq": f32(L0_attn_bq).reshape(1, DM), "bk": f32(L0_attn_bk).reshape(1, DM),
        "bv": f32(L0_attn_bv).reshape(1, DM), "bo": f32(L0_attn_bo).reshape(1, DM),
        "gamma": f32(L0_attn_gamma).reshape(1, DM),
        "beta": f32(L0_attn_beta).reshape(1, DM),
        "gw": f32(L0_ffn_gate_w), "gb": f32(L0_ffn_gate_b).reshape(1, NEXP),
        "eb": f32(L0_ffn_expert_b), "ew": f32(L0_ffn_expert_w),
        "wmod": f32(L0_wmod),
    }
    xs = f32(L0_attn_x)
    cs = f32(L0_c)
    in_maps = [dict(common, x=xs[b], c=cs[b:b + 1]) for b in range(NCORE)]
    res = run_bass_kernel_spmd(nc, in_maps, core_ids=list(range(NCORE)))
    return np.stack([res.results[b]["out"] for b in range(NCORE)], axis=0)
